# version 05
# baseline (speedup 1.0000x reference)
import contextlib

import numpy as np
import ml_dtypes

import concourse.bass as bass
import concourse.mybir as mybir
from concourse.bass_utils import run_bass_kernel_spmd

F32 = mybir.dt.float32
BF16 = mybir.dt.bfloat16
I32 = mybir.dt.int32
AF = mybir.ActivationFunctionType
ALU = mybir.AluOpType
AXX = mybir.AxisListType.X

B, T, DM = 4, 8192, 1024
EH, NST, RK = 1024, 16, 64
FT = DM // 128
NDT = EH // 128
TC = 512
NCH1 = T // TC
OC = 2048
NOC = T // OC
TH = T // 2
TC3 = 256
NCH3 = T // TC3
POOL_N = 11
EPS = 1e-5
MAGIC = 0x5F3759DF
RG = [[0, 1], [2, 3], [4, 5], [6, 7]]
RACE_DETECT = False
DEBUG_TAPS = False
NEWTON_BYPASS = False


def build_program():
    nc = bass.Bass("TRN2", target_bir_lowering=False,
                   detect_race_conditions=RACE_DETECT)

    x_in = nc.dram_tensor("x", [T, DM], F32, kind="ExternalInput")
    w_in_ = nc.dram_tensor("w_in", [DM, 2 * EH], F32, kind="ExternalInput")
    convd_in = nc.dram_tensor("conv_diag", [NDT * 4, 128, 128], F32, kind="ExternalInput")
    prm_in = nc.dram_tensor("prm", [128, 64], F32, kind="ExternalInput")
    alog_in = nc.dram_tensor("a_log", [128, 128], F32, kind="ExternalInput")
    xpw_in = nc.dram_tensor("x_proj_w", [EH, 96], F32, kind="ExternalInput")
    dtw_in = nc.dram_tensor("dt_w", [RK, EH], F32, kind="ExternalInput")
    wout_in = nc.dram_tensor("w_out", [2 * EH, DM], F32, kind="ExternalInput")
    wr_in = nc.dram_tensor("w_r", [DM, DM], F32, kind="ExternalInput")
    wk_in = nc.dram_tensor("w_k", [DM, DM], F32, kind="ExternalInput")
    wv_in = nc.dram_tensor("w_v", [DM, DM], F32, kind="ExternalInput")
    cvec_in = nc.dram_tensor("cvec", [128, FT], F32, kind="ExternalInput")
    wmod_in = nc.dram_tensor("wmod", [DM, 6 * DM], F32, kind="ExternalInput")
    ident_in = nc.dram_tensor("ident", [128, 128], F32, kind="ExternalInput")
    magic_in = nc.dram_tensor("magic", [128, 512], I32, kind="ExternalInput")
    out_t = nc.dram_tensor("out", [T, DM], F32, kind="ExternalOutput")
    dbg = {}
    if DEBUG_TAPS:
        for nm, shape, dt_ in (("d_mod", [1, 6 * DM], F32),
                               ("d_xdbl_in", [96, T], F32),
                               ("d_xdbl_out", [96, T], F32),
                               ("d_xi", [EH, T], BF16),
                               ("d_sr", [EH, T], BF16),
                               ("d_gin", [EH, T], BF16),
                               ("d_gout", [2, EH, T], BF16),
                               ("d_mod_early", [1, 6 * DM], F32),
                               ("t_h", [128, FT * TC], F32),
                               ("t_xn", [128, FT * TC], BF16),
                               ("t_xz", [128, NDT * (TC + 3)], BF16),
                               ("t_stats", [128, 32], F32),
                               ("t_rmsi", [1, TC], F32),
                               ("t_xln", [128, 4 * 1024], F32),
                               ("t_magic", [128, 512], I32),
                               ("t_nri", [128, 512], I32)):
            dbg[nm] = nc.dram_tensor(nm, shape, dt_, kind="ExternalOutput")

    spill_xi = nc.dram_tensor("spill_xi", [EH, T], BF16, kind="Internal")
    spill_sr = nc.dram_tensor("spill_sr", [EH, T], BF16, kind="Internal")
    cc_xdbl_in = nc.dram_tensor("cc_xdbl_in", [96, T], F32, kind="Internal")
    cc_xdbl_out = nc.dram_tensor("cc_xdbl_out", [96, T], F32, kind="Internal")
    bc16buf = nc.dram_tensor("bc16buf", [NOC, 32, OC], BF16, kind="Internal")
    cc_g_in = nc.dram_tensor("cc_g_in", [EH, T], BF16, kind="Internal")
    cc_g_out = nc.dram_tensor("cc_g_out", [2, EH, T], BF16, kind="Internal")
    modrow_d = nc.dram_tensor("modrow_d", [1, 6 * DM], F32, kind="Internal")

    ges = contextlib.ExitStack()

    def mk_sb(stk):
        def sb(name, shape, dt):
            return stk.enter_context(nc.sbuf_tensor(name, shape, dt))
        return sb

    gsb = mk_sb(ges)

    class Sem:
        __slots__ = ("h", "n")

        def __init__(self, name):
            self.h = ges.enter_context(nc.semaphore(name))
            self.n = 0

        def inc(self, instr, k=1):
            instr.then_inc(self.h, k)
            self.n += k
            return self.n

    class Bank:

        def __init__(self, name):
            self.t = ges.enter_context(nc.psum_tensor(name, [128, 512], F32))
            self.sp = Sem(name + "p")
            self.sf = Sem(name + "f")
            self.data = None
            self.free = None

        def begin(self, eng):
            if self.free is not None:
                eng.wait_ge(self.free[0], self.free[1])

        def produced(self, instr):
            self.sp.inc(instr)
            self.data = (self.sp.h, self.sp.n)

        def set_data(self, sem):
            self.data = (sem.h, sem.n)

        def wait_data(self, eng):
            eng.wait_ge(self.data[0], self.data[1])

        def done(self, instr):
            self.sf.inc(instr)
            self.free = (self.sf.h, self.sf.n)

        def set_free(self, sem):
            self.free = (sem.h, sem.n)

    V, A, PE, PL, SY = nc.vector, nc.scalar, nc.tensor, nc.gpsimd, nc.sync

    dtw16 = gsb("dtw16", [RK, EH], BF16)
    xpw16 = gsb("xpw16", [128, FT * 96], BF16)
    prm = gsb("prm_s", [128, 64], F32)
    magic = gsb("magic_s", [128, 512], I32)
    identf = gsb("identf", [128, 128], F32)
    ident16 = gsb("ident16", [128, 128], BF16)
    ones16 = gsb("ones16", [128, 1], BF16)
    onesf_row = gsb("onesf_row", [1, 128], F32)
    asc = gsb("asc", [128, 128], F32)
    state = gsb("state", [128, NDT * NST], F32)
    modp = gsb("modp", [128, 6 * FT], F32)
    tiny = gsb("tiny", [128, 64], F32)
    tiny2 = gsb("tiny2", [128, 64], F32)
    nr_i = gsb("nr_i", [128, 512], I32)
    nr_f = gsb("nr_f", [128, 512], F32)

    banks = [Bank(f"bk{i}") for i in range(8)]

    sPRO = Sem("sPRO")
    sCC = Sem("sCC")
    sA2A = Sem("sA2A")

    sVL = Sem("sVL")
    sDBGG = Sem("sDBGG") if DEBUG_TAPS else None

    def vlock(instr):
        sVL.inc(instr)
        V.wait_ge(sVL.h, sVL.n)

    def newton_rsqrt(dst, src, p, n, it=3):
        if NEWTON_BYPASS:
            V.tensor_single_scalar(dst, src, 1.0, ALU.min)
            return
        tiny_op = n < 128
        bits = nr_i[0:p, 0:n]
        i = V.tensor_single_scalar(bits, src.bitcast(I32), 1, ALU.arith_shift_right)
        if tiny_op:
            vlock(i)
        i = V.tensor_tensor(bits, magic[0:p, 0:n], bits, ALU.subtract)
        if tiny_op:
            vlock(i)
        i = V.tensor_copy(dst, bits.bitcast(F32))
        if tiny_op:
            vlock(i)
        for _ in range(it):
            t = nr_f[0:p, 0:n]
            i = V.tensor_tensor(t, dst, dst, ALU.mult)
            if tiny_op:
                vlock(i)
            i = V.tensor_tensor(t, t, src, ALU.mult)
            if tiny_op:
                vlock(i)
            i = V.tensor_scalar(t, t, -0.5, 1.5, ALU.mult, ALU.add)
            if tiny_op:
                vlock(i)
            i = V.tensor_tensor(dst, dst, t, ALU.mult)
            if tiny_op:
                vlock(i)

    def emit_prologue():
        ps = contextlib.ExitStack()
        sb = mk_sb(ps)
        sLD = Sem("sLD")
        SY.dma_start(out=prm[:], in_=prm_in.ap()).then_inc(sLD.h, 16)
        SY.dma_start(out=magic[:], in_=magic_in.ap()).then_inc(sLD.h, 16)
        SY.dma_start(out=identf[:], in_=ident_in.ap()).then_inc(sLD.h, 16)
        SY.dma_start(out=tiny[:, 0:FT], in_=cvec_in.ap()).then_inc(sLD.h, 16)
        dtw_f = sb("dtw_f", [RK, EH], F32)
        SY.dma_start(out=dtw_f[:], in_=dtw_in.ap()).then_inc(sLD.h, 16)
        xpw_f = sb("xpw_f", [128, FT * 96], F32)
        SY.dma_start(out=xpw_f[:].rearrange("p (k n) -> p k n", k=FT),
                     in_=bass.AP(xpw_in, 0, [[96, 128], [96 * 128, FT], [1, 96]])
                     ).then_inc(sLD.h, 16)
        sLD.n = 6 * 16
        A.wait_ge(sLD.h, sLD.n)
        A.copy(dtw16[:], dtw_f[:])
        A.copy(xpw16[:], xpw_f[:])
        A.copy(ident16[:], identf[:])
        i = A.activation(tiny2[:, 0:FT], tiny[:, 0:FT], AF.Silu)
        sSIL = Sem("sSIL")
        sSIL.inc(i)
        sPRO.inc(i) if False else None
        V.memset(ones16[:], 1.0)
        i = V.memset(onesf_row[:], 1.0)
        sPRO.inc(i)
        i = V.memset(state[:], 0.0)
        sPRO.inc(i)

        siluc = sb("siluc", [128, FT], BF16)
        V.wait_ge(sSIL.h, 1)
        i = V.tensor_copy(siluc[:], tiny2[:, 0:FT])
        sPRO.inc(i)

        wst = sb("wst", [128, 1536], F32)
        wst16 = sb("wst16", [128, 1536], BF16)
        modrow = sb("modrow", [1, 6 * DM], F32)
        sWM = Sem("sWM")
        sWC = Sem("sWC")
        sMM = Sem("sMM")
        sWMM = Sem("sWMM")
        for g in range(4):
            for kt in range(FT):
                step = g * FT + kt
                if step >= 1:
                    SY.wait_ge(sWC.h, step)
                SY.dma_start(out=wst[:],
                             in_=bass.AP(wmod_in, kt * 6 * DM * 128 + g * 1536,
                                         [[6 * DM, 128], [1, 1536]])
                             ).then_inc(sWM.h, 16)
                sWM.n += 16
                A.wait_ge(sWM.h, sWM.n)
                if kt == 0:
                    if g >= 1:
                        A.wait_ge(sMM.h, 3 * g)
                else:
                    A.wait_ge(sWMM.h, 7 * g + kt)
                i = A.copy(wst16[:], wst[:])
                sWC.inc(i)
                PE.wait_ge(sWC.h, sWC.n)
                if step == 0:
                    PE.wait_ge(sPRO.h, 3)
                for s3 in range(3):
                    bk = banks[s3]
                    if kt == 0:
                        bk.begin(PE)
                    i = PE.matmul(bk.t[0:1, 0:512], siluc[:, kt:kt + 1],
                                  wst16[:, s3 * 512:(s3 + 1) * 512],
                                  start=(kt == 0), stop=(kt == FT - 1))
                    if kt == FT - 1:
                        bk.produced(i)
                        bk.wait_data(V)
                        i = V.tensor_copy(
                            modrow[0:1, g * 1536 + s3 * 512:g * 1536 + (s3 + 1) * 512],
                            bk.t[0:1, 0:512])
                        sMM.inc(i)
                        bk.set_free(sMM)
                    elif s3 == 2:
                        sWMM.inc(i)
        sMR = Sem("sMR")
        SY.wait_ge(sMM.h, sMM.n)
        SY.dma_start(out=modrow_d.ap(), in_=modrow[:]).then_inc(sMR.h, 16)
        SY.wait_ge(sMR.h, 16)
        SY.dma_start(out=modp[:].rearrange("p (j f) -> p j f", j=6),
                     in_=bass.AP(modrow_d, 0, [[1, 128], [DM, 6], [128, FT]]),
                     allow_slow_non_contiguous=True).then_inc(sMR.h, 16)
        V.wait_ge(sMR.h, 32)
        V.tensor_scalar_add(modp[:, 1 * FT:2 * FT], modp[:, 1 * FT:2 * FT], 1.0)
        V.tensor_scalar_add(modp[:, 4 * FT:5 * FT], modp[:, 4 * FT:5 * FT], 1.0)
        sGAB = Sem("sGAB")
        A.wait_ge(sMR.h, 32)
        for ci, col in enumerate((2, 5)):
            sl = modp[:, col * FT:(col + 1) * FT]
            ab = tiny[:, 8 + ci * FT:8 + (ci + 1) * FT]
            sg = tiny2[:, 8 + ci * FT:8 + (ci + 1) * FT]
            i = A.activation(ab, sl, AF.Abs)
            i = A.activation(sg, sl, AF.Sign)
            sGAB.inc(i)
            V.wait_ge(sGAB.h, sGAB.n)
            i = V.tensor_scalar_max(ab, ab, 0.25)
            vlock(i)
            i = V.tensor_tensor(sl, ab, sg, ALU.mult)
        sPRO.inc(i)
        if DEBUG_TAPS:
            sMT = sDBGG
            SY.wait_ge(sMR.h, 32)
            i = SY.dma_start(out=dbg["d_mod_early"].ap(), in_=modrow_d.ap())
            sMT.inc(i, 16)
            SY.wait_ge(sMT.h, 16)
        SY.wait_ge(sMR.h, 32)
        SY.wait_ge(sPRO.h, 4)
        SY.wait_ge(sGAB.h, sGAB.n)
        nc.all_engine_barrier()
        ps.close()

    def emit_phase1():
        ps = contextlib.ExitStack()
        sb = mk_sb(ps)
        win16 = sb("win16", [128, FT * 2048], BF16)
        convd16 = sb("convd16", [128, NDT * 512], BF16)
        xtm = [sb(f"xtm{i}", [128, 4 * 1024], F32) for i in range(2)]
        xln = sb("xln", [128, 4 * 1024], F32)
        hfm = sb("hfm", [128, FT * TC], F32)
        xn16 = sb("xn16", [128, FT * TC], BF16)
        xz16 = sb("xz16", [128, NDT * (TC + 3)], BF16)
        xi16c = [sb(f"xi16c{i}", [128, NDT * TC], BF16) for i in range(2)]
        sr16c = [sb(f"sr16c{i}", [128, NDT * TC], BF16) for i in range(2)]
        hh16 = [sb(f"hh16{i}", [128, TC], BF16) for i in range(2)]
        xdst = [sb(f"xdst{i}", [96, TC], F32) for i in range(2)]
        scr = sb("scr1", [128, 1024], F32)
        stats = sb("stats1", [128, 32], F32)
        rmsi = sb("rmsi", [1, TC], F32)
        wstage = sb("wstage1", [128, 2048], F32)
        cstage = sb("cstage", [128, 512], F32)

        sWS = Sem("sWS"); sWL = Sem("sWL")
        sXL = Sem("p1xl"); sXFV = Sem("p1xfv")
        sSQ = Sem("p1sq"); sINV = Sem("p1inv"); sXLN = Sem("p1xln")
        sHEV = Sem("p1hev"); sHH = Sem("p1hh"); sONE = Sem("p1one")
        sRM = Sem("p1rm"); sXN = Sem("p1xn")
        sXZ = Sem("p1xz"); sHC = Sem("p1hc"); sCV = Sem("p1cv")
        sXI = Sem("p1xi"); sSR = Sem("p1sr")
        sXID = Sem("p1xid"); sSRD = Sem("p1srd")
        sXPD = Sem("p1xpd"); sXDE = Sem("p1xde"); sXDD = Sem("p1xdd")

        for kt in range(FT):
            if kt >= 1:
                SY.wait_ge(sWL.h, kt)
            SY.dma_start(out=wstage[:],
                         in_=bass.AP(w_in_, kt * 128 * 2 * EH, [[2 * EH, 128], [1, 2048]])
                         ).then_inc(sWS.h, 16)
            sWS.n += 16
            A.wait_ge(sWS.h, sWS.n)
            i = A.copy(win16[:, kt * 2048:(kt + 1) * 2048], wstage[:])
            sWL.inc(i)
        for dt in range(NDT):
            SY.wait_ge(sWL.h, FT + dt)
            SY.dma_start(out=cstage[:].rearrange("p (t n) -> p t n", t=4),
                         in_=bass.AP(convd_in, dt * 4 * 128 * 128,
                                     [[128, 128], [128 * 128, 4], [1, 128]])
                         ).then_inc(sWS.h, 16)
            sWS.n += 16
            A.wait_ge(sWS.h, sWS.n)
            i = A.copy(convd16[:, dt * 512:(dt + 1) * 512], cstage[:])
            sWL.inc(i)

        for dt in range(NDT):
            i = V.memset(xz16[:, dt * (TC + 3):dt * (TC + 3) + 3], 0.0)
            sHC.inc(i)

        for c in range(NCH1):
            p = c % 2
            so = 16 * p
            if c >= 2:
                SY.wait_ge(sXLN.h, 4 * (c - 1))
                SY.wait_ge(sSQ.h, 4 * (c - 1))
                SY.wait_ge(sXFV.h, c - 1)
            SY.dma_start(out=xtm[p][:].rearrange("p (s d) -> p s d", s=4),
                         in_=bass.AP(x_in, c * TC * DM,
                                     [[DM, 128], [128 * DM, 4], [1, DM]])
                         ).then_inc(sXL.h, 16)
            sXL.n += 16

            V.wait_ge(sXL.h, sXL.n)
            A.wait_ge(sXL.h, sXL.n)
            for s in range(4):
                xs = xtm[p][:, s * 1024:(s + 1) * 1024]
                i = V.tensor_reduce(stats[:, so + s:so + s + 1], xs, AXX, ALU.add)
                if s == 3:
                    sXFV.inc(i)
                i = A.activation(scr[:], xs, AF.Square,
                                 accum_out=stats[:, so + 4 + s:so + 5 + s])
                sSQ.inc(i)
            V.wait_ge(sSQ.h, sSQ.n)
            if c >= 1:
                V.wait_ge(sXLN.h, 4 * c)
            m4 = stats[:, so + 8:so + 12]
            v4 = stats[:, so + 12:so + 16]
            i = V.tensor_scalar_mul(m4, stats[:, so:so + 4], 1.0 / DM)
            vlock(i)
            i = V.tensor_scalar_mul(v4, stats[:, so + 4:so + 8], 1.0 / DM)
            vlock(i)
            i = V.tensor_tensor(tiny[:, 0:4], m4, m4, ALU.mult)
            vlock(i)
            i = V.tensor_tensor(v4, v4, tiny[:, 0:4], ALU.subtract)
            vlock(i)
            i = V.tensor_scalar_add(v4, v4, EPS)
            vlock(i)
            inv4 = stats[:, so:so + 4]
            newton_rsqrt(inv4, v4, 128, 4)
            nm4 = stats[:, so + 4:so + 8]
            i = V.tensor_scalar_mul(nm4, m4, -1.0)
            vlock(i)
            i = V.tensor_tensor(nm4, nm4, inv4, ALU.mult)
            sINV.inc(i)

            A.wait_ge(sINV.h, sINV.n)
            if c >= 1:
                A.wait_ge(sHEV.h, FT * c)
            for s in range(4):
                i = A.activation(xln[:, s * 1024:(s + 1) * 1024],
                                 xtm[p][:, s * 1024:(s + 1) * 1024], AF.Identity,
                                 bias=stats[:, so + 4 + s:so + 5 + s],
                                 scale=stats[:, so + s:so + s + 1])
                sXLN.inc(i)

            PE.wait_ge(sXLN.h, sXLN.n)
            if c == 0:
                PE.wait_ge(sPRO.h, 4)
            for j in range(FT):
                bk = banks[j % 2]
                bk.begin(PE)
                for s in range(4):
                    i = PE.transpose(bk.t[:, s * 128:(s + 1) * 128],
                                     xln[:, s * 1024 + j * 128:s * 1024 + (j + 1) * 128],
                                     identf[:])
                bk.produced(i)
                bk.wait_data(A)
                if c >= 1:
                    A.wait_ge(sXN.h, FT * c)
                i = A.activation(hfm[:, j * TC:(j + 1) * TC], bk.t[:], AF.Identity,
                                 bias=modp[:, 0 * FT + j:0 * FT + j + 1],
                                 scale=modp[:, 1 * FT + j:1 * FT + j + 1])
                sHEV.inc(i)
                bk.set_free(sHEV)

            rbk = banks[7]
            rbk.begin(PE)
            for j in range(FT):
                A.wait_ge(sHEV.h, sHEV.n - FT + j + 1)
                if 8 * c + j >= 2:
                    A.wait_ge(sONE.h, 8 * c + j - 1)
                i = A.activation(hh16[j % 2][:], hfm[:, j * TC:(j + 1) * TC], AF.Square)
                sHH.inc(i)
                PE.wait_ge(sHH.h, sHH.n)
                i = PE.matmul(rbk.t[0:1, 0:TC], ones16[:], hh16[j % 2][:],
                              start=(j == 0), stop=(j == FT - 1))
                sONE.inc(i)
            rbk.set_data(sONE)
            rbk.wait_data(V)
            V.tensor_scalar(rmsi[0:1, :], rbk.t[0:1, 0:TC], 1.0 / DM, EPS,
                            ALU.mult, ALU.add)
            newton_rsqrt(rmsi[0:1, :], rmsi[0:1, :], 1, TC)
            i = V.tensor_copy(tiny[0:1, 32:33], rmsi[0:1, 0:1])
            sRM.inc(i)
            rbk.set_free(sRM)
            bbk = banks[6]
            bbk.begin(PE)
            PE.wait_ge(sRM.h, sRM.n)
            i = PE.matmul(bbk.t[:, 0:TC], onesf_row[:], rmsi[0:1, :],
                          start=True, stop=True)
            bbk.produced(i)
            bbk.wait_data(V)
            if c >= 1:
                V.wait_ge(sXZ.h, NDT * c)
                V.wait_ge(sSR.h, NDT * c)
            for j in range(FT):
                i = V.scalar_tensor_tensor(xn16[:, j * TC:(j + 1) * TC],
                                           hfm[:, j * TC:(j + 1) * TC],
                                           prm[:, 24 + j:25 + j],
                                           bbk.t[:, 0:TC], ALU.mult, ALU.mult)
                sXN.inc(i)
            bbk.set_free(sXN)

            PE.wait_ge(sXN.h, sXN.n)
            if c == 0:
                PE.wait_ge(sWL.h, FT + NDT)
            for m in range(16):
                bk = banks[2 + (m % 2)]
                bk.begin(PE)
                for kt in range(FT):
                    i = PE.matmul(bk.t[:, 0:TC],
                                  win16[:, kt * 2048 + m * 128:kt * 2048 + (m + 1) * 128],
                                  xn16[:, kt * TC:(kt + 1) * TC],
                                  start=(kt == 0), stop=(kt == FT - 1))
                bk.produced(i)
                bk.wait_data(A)
                if m < NDT:
                    dt = m
                    A.wait_ge(sHC.h, 8 * (c + 1))
                    i = A.activation(xz16[:, dt * (TC + 3) + 3:(dt + 1) * (TC + 3)],
                                     bk.t[:, 0:TC], AF.Copy)
                    sXZ.inc(i)
                    bk.set_free(sXZ)
                else:
                    dt = m - NDT
                    if c >= 2:
                        A.wait_ge(sSRD.h, 16 * (c - 1))
                    i = A.activation(sr16c[p][:, dt * TC:(dt + 1) * TC],
                                     bk.t[:, 0:TC], AF.Silu)
                    sSR.inc(i)
                    bk.set_free(sSR)

            for dt in range(NDT):
                bk = banks[4 + (dt % 2)]
                bk.begin(PE)
                PE.wait_ge(sXZ.h, sXZ.n - NDT + dt + 1)
                for tap in range(4):
                    i = PE.matmul(bk.t[:, 0:TC],
                                  convd16[:, dt * 512 + tap * 128:dt * 512 + (tap + 1) * 128],
                                  xz16[:, dt * (TC + 3) + tap:dt * (TC + 3) + tap + TC],
                                  start=(tap == 0), stop=(tap == 3))
                sCV.inc(i)
                bk.set_data(sCV)
                bk.wait_data(A)
                if c >= 2:
                    A.wait_ge(sXID.h, 16 * (c - 1))
                i = A.activation(xi16c[p][:, dt * TC:(dt + 1) * TC], bk.t[:, 0:TC],
                                 AF.Silu, bias=prm[:, 0 + dt:1 + dt])
                sXI.inc(i)
                bk.set_free(sXI)
                V.wait_ge(sCV.h, sCV.n)
                i = V.tensor_copy(xz16[:, dt * (TC + 3):dt * (TC + 3) + 3],
                                  xz16[:, dt * (TC + 3) + TC:dt * (TC + 3) + TC + 3])
                sHC.inc(i)

            xbk = banks[2 + (c % 2)]
            xbk.begin(PE)
            PE.wait_ge(sXI.h, sXI.n)
            for kt in range(NDT):
                i = PE.matmul(xbk.t[0:96, 0:TC], xpw16[:, kt * 96:(kt + 1) * 96],
                              xi16c[p][:, kt * TC:(kt + 1) * TC],
                              start=(kt == 0), stop=(kt == NDT - 1))
            sXPD.inc(i)
            xbk.set_data(sXPD)
            xbk.wait_data(V)
            if c >= 2:
                V.wait_ge(sXDD.h, 16 * (c - 1))
            i = V.tensor_copy(xdst[p][0:96, :], xbk.t[0:96, 0:TC])
            sXDE.inc(i)
            xbk.set_free(sXDE)

            A.wait_ge(sXI.h, sXI.n)
            i = A.dma_start(out=bass.AP(spill_xi, c * TC,
                                        [[T, 128], [128 * T, NDT], [1, TC]]),
                            in_=xi16c[p][:].rearrange("p (d t) -> p d t", d=NDT))
            sXID.inc(i, 16)
            A.wait_ge(sSR.h, sSR.n)
            i = A.dma_start(out=bass.AP(spill_sr, c * TC,
                                        [[T, 128], [128 * T, NDT], [1, TC]]),
                            in_=sr16c[p][:].rearrange("p (d t) -> p d t", d=NDT))
            sSRD.inc(i, 16)
            SY.wait_ge(sXDE.h, sXDE.n)
            i = SY.dma_start(out=bass.AP(cc_xdbl_in, c * TC, [[T, 96], [1, TC]]),
                             in_=xdst[p][0:96, :])
            sXDD.inc(i, 16)
            if DEBUG_TAPS and c == 0:
                sT1 = sDBGG
                SY.wait_ge(sXI.h, NDT)
                SY.wait_ge(sSR.h, NDT)
                for nm, srcap in (("t_h", hfm[:]), ("t_xn", xn16[:]),
                                  ("t_xz", xz16[:]), ("t_stats", stats[:]),
                                  ("t_rmsi", rmsi[:]), ("t_xln", xln[:]),
                                  ("t_magic", magic[:]), ("t_nri", nr_i[:])):
                    i = SY.dma_start(out=dbg[nm].ap(), in_=srcap)
                    sT1.inc(i, 16)
                SY.wait_ge(sT1.h, sT1.n)

        PL.wait_ge(sXDD.h, 16 * NCH1)
        i = PL.collective_compute("AllReduce", ALU.add, replica_groups=RG,
                                  ins=[cc_xdbl_in.ap()], outs=[cc_xdbl_out.ap()])
        sCC.inc(i)
        ps.close()
        return dict(sXID=sXID, sSRD=sSRD, sXDD=sXDD, sXDE=sXDE, sXPD=sXPD,
                    sXI=sXI, sSR=sSR)

    def emit_phase2(p1):
        for eng in (SY, A, V, PE, PL):
            eng.wait_ge(sCC.h, 1)
            eng.wait_ge(p1["sXID"].h, p1["sXID"].n)
            eng.wait_ge(p1["sSRD"].h, p1["sSRD"].n)
            eng.wait_ge(p1["sXDD"].h, p1["sXDD"].n)
            eng.wait_ge(p1["sXDE"].h, p1["sXDE"].n)
            eng.wait_ge(p1["sXPD"].h, p1["sXPD"].n)

        ps = contextlib.ExitStack()
        sb = mk_sb(ps)
        delta = [sb(f"delta{i}", [128, OC], F32) for i in range(2)]
        du16 = [sb(f"du16{i}", [128, OC], BF16) for i in range(2)]
        xi16s = [sb(f"xi16s{i}", [128, OC], BF16) for i in range(2)]
        sr16s = [sb(f"sr16s{i}", [128, OC], BF16) for i in range(2)]
        dab = [sb(f"dab{i}", [128, OC], F32) for i in range(2)]
        dbub = [sb(f"dbub{i}", [128, OC], BF16) for i in range(2)]
        hb = [sb(f"hb{i}", [128, OC], BF16) for i in range(2)]
        z16b = [sb(f"z16b{i}", [128, OC], BF16) for i in range(2)]
        bbc = [sb(f"bbc{i}", [128, OC], BF16) for i in range(2)]
        cbc = [sb(f"cbc{i}", [128, OC], BF16) for i in range(2)]
        g16b = [sb(f"g16b{i}", [128, OC], BF16) for i in range(2)]
        xdblk = sb("xdblk", [RK, OC], F32)
        xdblk16 = sb("xdblk16", [RK, OC], BF16)
        bcst = sb("bcst", [32, OC], F32)
        bcst16 = sb("bcst16", [32, OC], BF16)
        dpre = sb("dpre", [128, OC], F32)
        aabs = sb("aabs", [128, OC], F32)
        sexp = sb("sexp", [128, OC], F32)
        sln = sb("sln", [128, OC], F32)
        gt1 = sb("gt1", [128, 512], F32)
        alog2 = sb("alog2", [128, 128], F32)

        sASC = Sem("p2asc")
        sXDL = Sem("p2xdl"); sBCL = Sem("p2bcl"); sBCC = Sem("p2bcc"); sBCD = Sem("p2bcd")
        sXDK = Sem("p2xdk"); sDTP = Sem("p2dtp"); sSPE = Sem("p2spe"); sSPL = Sem("p2spl")
        sDEL = Sem("p2del"); sXI2 = Sem("p2xi2"); sSR2 = Sem("p2sr2"); sDU = Sem("p2du")
        sBB = Sem("p2bb"); sCB = Sem("p2cb"); sDA = Sem("p2da")
        sDBv = Sem("p2dbv"); sDBp = Sem("p2dbp"); sZv = Sem("p2zv"); sZp = Sem("p2zp")
        dbu_rec = {}; z_rec = {}
        sSC = Sem("p2sc"); sY = Sem("p2y"); sGT = Sem("p2gt")
        sGD = Sem("p2gd")

        SY.dma_start(out=alog2[:], in_=alog_in.ap()).then_inc(sASC.h, 16)
        sASC.n += 16
        A.wait_ge(sASC.h, sASC.n)
        i = A.activation(asc[:], alog2[:], AF.Exp)
        sASC.inc(i)
        V.wait_ge(sASC.h, sASC.n)
        i = V.tensor_scalar_mul(asc[:], asc[:], -1.0)
        sASC.inc(i)

        it = 0
        for oc in range(NOC):
            if oc >= 1:
                SY.wait_ge(sDTP.h, 4 * NDT * oc)
            SY.dma_start(out=xdblk[:],
                         in_=bass.AP(cc_xdbl_out, oc * OC, [[T, RK], [1, OC]])
                         ).then_inc(sXDL.h, 16)
            sXDL.n += 16
            if oc >= 1:
                SY.wait_ge(sBCD.h, 16 * oc)
            SY.dma_start(out=bcst[:],
                         in_=bass.AP(cc_xdbl_out, RK * T + oc * OC, [[T, 32], [1, OC]])
                         ).then_inc(sBCL.h, 16)
            sBCL.n += 16
            V.wait_ge(sBCL.h, sBCL.n)
            i = V.tensor_copy(bcst16[:], bcst[:])
            sBCC.inc(i)
            SY.wait_ge(sBCC.h, sBCC.n)
            i = SY.dma_start(out=bass.AP(bc16buf, oc * 32 * OC, [[OC, 32], [1, OC]]),
                             in_=bcst16[:])
            sBCD.inc(i, 16)
            V.wait_ge(sXDL.h, sXDL.n)
            if oc >= 1:
                V.wait_ge(sDTP.h, 4 * NDT * oc)
            i = V.tensor_copy(xdblk16[:], xdblk[:])
            sXDK.inc(i)

            for dt in range(NDT):
                dtg = oc * NDT + dt
                pp = dt % 2
                bset = banks[0:4] if pp == 0 else banks[4:8]
                PE.wait_ge(sXDK.h, oc + 1)
                for sub in range(4):
                    bk = bset[sub]
                    bk.begin(PE)
                    i = PE.matmul(bk.t[:, 0:512], dtw16[:, dt * 128:(dt + 1) * 128],
                                  xdblk16[:, sub * 512:(sub + 1) * 512],
                                  start=True, stop=True)
                    sDTP.inc(i)
                    bk.set_data(sDTP)
                if dtg >= 1:
                    A.wait_ge(sDEL.h, dtg)
                for sub in range(4):
                    bk = bset[sub]
                    bk.wait_data(A)
                    i = A.activation(aabs[:, sub * 512:(sub + 1) * 512], bk.t[:, 0:512],
                                     AF.Abs, bias=prm[:, 8 + dt:9 + dt])
                    i = A.activation(dpre[:, sub * 512:(sub + 1) * 512], bk.t[:, 0:512],
                                     AF.Identity, bias=prm[:, 8 + dt:9 + dt])
                    sSPE.inc(i)
                    bk.set_free(sSPE)
                i = A.activation(sexp[:], aabs[:], AF.Exp, scale=-1.0)
                i = A.activation(sln[:], sexp[:], AF.Ln, bias=1.0)
                sSPL.inc(i)
                V.wait_ge(sSPL.h, sSPL.n)
                if dtg >= 2:
                    V.wait_ge(sDA.h, (dtg - 1) * NST)
                i = V.scalar_tensor_tensor(delta[pp][:], dpre[:], 0.0, sln[:],
                                           ALU.max, ALU.add)
                sDEL.inc(i)
                if dtg >= 2:
                    SY.wait_ge(sDU.h, dtg - 1)
                    SY.wait_ge(sGT.h, 4 * (dtg - 1))
                SY.dma_start(out=xi16s[pp][:],
                             in_=bass.AP(spill_xi, dt * 128 * T + oc * OC,
                                         [[T, 128], [1, OC]])
                             ).then_inc(sXI2.h, 16)
                sXI2.n += 16
                SY.dma_start(out=sr16s[pp][:],
                             in_=bass.AP(spill_sr, dt * 128 * T + oc * OC,
                                         [[T, 128], [1, OC]])
                             ).then_inc(sSR2.h, 16)
                sSR2.n += 16
                V.wait_ge(sXI2.h, sXI2.n)
                i = V.tensor_tensor(du16[pp][:], delta[pp][:], xi16s[pp][:], ALU.mult)
                sDU.inc(i)

                for n in range(NST):
                    np_ = it % 2
                    if it >= 2:
                        SY.wait_ge(*dbu_rec[it - 2])
                    if n == 0:
                        SY.wait_ge(sBCD.h, 16 * (oc + 1))
                    SY.dma_start(out=bbc[np_][:],
                                 in_=bass.AP(bc16buf, (oc * 32 + n) * OC,
                                             [[0, 128], [1, OC]])
                                 ).then_inc(sBB.h, 16)
                    sBB.n += 16
                    if it >= 2:
                        SY.wait_ge(*z_rec[it - 2])
                    SY.dma_start(out=cbc[np_][:],
                                 in_=bass.AP(bc16buf, (oc * 32 + 16 + n) * OC,
                                             [[0, 128], [1, OC]])
                                 ).then_inc(sCB.h, 16)
                    sCB.n += 16
                    A.wait_ge(sDEL.h, dtg + 1)
                    if it >= 2:
                        A.wait_ge(sSC.h, it - 1)
                    i = A.activation(dab[np_][:], delta[pp][:], AF.Exp,
                                     scale=asc[:, dt * NST + n:dt * NST + n + 1])
                    sDA.inc(i)
                    eng = PL if n < POOL_N else V
                    sdb = sDBp if n < POOL_N else sDBv
                    eng.wait_ge(sDU.h, dtg + 1)
                    eng.wait_ge(sBB.h, sBB.n)
                    if it >= 2:
                        eng.wait_ge(sSC.h, it - 1)
                    i = eng.tensor_tensor(dbub[np_][:], du16[pp][:], bbc[np_][:], ALU.mult)
                    sdb.inc(i)
                    dbu_rec[it] = (sdb.h, sdb.n)
                    V.wait_ge(sDA.h, it + 1)
                    V.wait_ge(*dbu_rec[it])
                    if it >= 2:
                        V.wait_ge(*z_rec[it - 2])
                    i = V.tensor_tensor_scan(hb[np_][:], dab[np_][:], dbub[np_][:],
                                             state[:, dt * NST + n:dt * NST + n + 1],
                                             ALU.mult, ALU.add)
                    sSC.inc(i)
                    V.wait_ge(sSC.h, sSC.n)
                    V.tensor_copy(state[:, dt * NST + n:dt * NST + n + 1],
                                  hb[np_][:, OC - 1:OC])
                    enz = PL if n < POOL_N else V
                    sz = sZp if n < POOL_N else sZv
                    enz.wait_ge(sSC.h, it + 1)
                    enz.wait_ge(sCB.h, sCB.n)
                    if it >= 2:
                        enz.wait_ge(sY.h, it - 1)
                    i = enz.tensor_tensor(z16b[np_][:], hb[np_][:], cbc[np_][:], ALU.mult)
                    sz.inc(i)
                    z_rec[it] = (sz.h, sz.n)
                    PE.wait_ge(*z_rec[it])
                    if n == 0:
                        PE.wait_ge(sSPE.h, 4 * (dtg + 1))
                        for sub in range(4):
                            bset[sub].begin(PE)
                    for sub in range(4):
                        i = PE.matmul(bset[sub].t[:, 0:512], ident16[:],
                                      z16b[np_][:, sub * 512:(sub + 1) * 512],
                                      start=(n == 0), stop=(n == NST - 1))
                    sY.inc(i)
                    it += 1

                V.wait_ge(sY.h, it)
                V.wait_ge(sSR2.h, sSR2.n)
                if dtg >= 2:
                    V.wait_ge(sGD.h, 16 * (dtg - 1))
                for sub in range(4):
                    V.scalar_tensor_tensor(gt1[:], xi16s[pp][:, sub * 512:(sub + 1) * 512],
                                           prm[:, 16 + dt:17 + dt],
                                           bset[sub].t[:, 0:512], ALU.mult, ALU.add)
                    i = V.tensor_tensor(g16b[pp][:, sub * 512:(sub + 1) * 512], gt1[:],
                                        sr16s[pp][:, sub * 512:(sub + 1) * 512], ALU.mult)
                    sGT.inc(i)
                    bset[sub].set_free(sGT)
                A.wait_ge(sGT.h, 4 * (dtg + 1))
                i = A.dma_start(
                    out=bass.AP(cc_g_in, dt * 128 * T + oc * OC, [[T, 128], [1, OC]]),
                    in_=g16b[pp][:])
                sGD.inc(i, 16)

        PL.wait_ge(sGD.h, sGD.n)
        i = PL.collective_compute("AllGather", ALU.bypass, replica_groups=RG,
                                  ins=[cc_g_in.ap()], outs=[cc_g_out.ap()])
        sA2A.inc(i)
        ps.close()
        return dict(sGD=sGD, sGT=sGT, sY=sY, sZv=sZv, sZp=sZp, sSC=sSC)

    def emit_phase3(p2):
        for eng in (SY, A, V, PE, PL):
            eng.wait_ge(sA2A.h, 1)
            eng.wait_ge(p2["sGD"].h, p2["sGD"].n)
            eng.wait_ge(p2["sGT"].h, p2["sGT"].n)
            eng.wait_ge(p2["sY"].h, p2["sY"].n)
            eng.wait_ge(p2["sZv"].h, p2["sZv"].n)
            eng.wait_ge(p2["sZp"].h, p2["sZp"].n)

        ps = contextlib.ExitStack()
        sb = mk_sb(ps)
        wout16 = sb("wout16", [128, 16 * DM], BF16)
        wr16 = sb("wr16", [128, FT * DM], BF16)
        wk16 = sb("wk16", [128, FT * DM], BF16)
        wv16 = sb("wv16", [128, FT * DM], BF16)
        wstage = sb("wstage3", [128, DM], F32)
        sW3 = Sem("p3w"); sW3c = Sem("p3wc")
        for src, dst, nkt in ((wout_in, wout16, 16), (wr_in, wr16, FT),
                              (wk_in, wk16, FT), (wv_in, wv16, FT)):
            for kt in range(nkt):
                if sW3c.n >= 1:
                    SY.wait_ge(sW3c.h, sW3c.n)
                SY.dma_start(out=wstage[:],
                             in_=bass.AP(src, kt * 128 * DM, [[DM, 128], [1, DM]])
                             ).then_inc(sW3.h, 16)
                sW3.n += 16
                A.wait_ge(sW3.h, sW3.n)
                i = A.copy(dst[:, kt * DM:(kt + 1) * DM], wstage[:])
                sW3c.inc(i)

        g16 = sb("g16", [128, 16 * TC3], BF16)
        xtm3 = sb("xtm3", [128, 2 * 1024], F32)
        xln3 = sb("xln3", [128, 2 * 1024], F32)
        h3 = sb("h3", [128, FT * TC3], F32)
        xnew = sb("xnew", [128, FT * TC3], F32)
        xnew16 = sb("xnew16", [128, FT * TC3], BF16)
        sq16 = sb("sq16", [128, TC3], BF16)
        h2 = sb("h2", [128, FT * (TC3 + 1)], F32)
        xr16 = sb("xr16", [128, FT * TC3], BF16)
        xk16 = sb("xk16", [128, FT * TC3], BF16)
        r16 = sb("r16", [128, FT * TC3], BF16)
        kact16 = sb("kact16", [128, FT * TC3], BF16)
        fout = sb("fout", [128, FT * TC3], F32)
        outtm = sb("outtm", [128, 2 * 1024], F32)
        scr3 = sb("scr3", [128, 1024], F32)
        st3 = sb("st3", [128, 16], F32)
        ln2s = sb("ln2s", [1, 2 * TC3], F32)
        mixt = sb("mixt", [128, TC3], F32)

        sGL = Sem("p3gl"); sXL3 = Sem("p3xl"); sSQ3 = Sem("p3sq")
        sXFV3 = Sem("p3xfv"); sINV3 = Sem("p3inv")
        sXLN3 = Sem("p3xln"); sHEV3 = Sem("p3hev"); sOP = Sem("p3op")
        sAO = Sem("p3ao"); sXNW = Sem("p3xnw"); sXN16 = Sem("p3xn16")
        sSQF = Sem("p3sqf"); sMS = Sem("p3ms"); sLNI = Sem("p3lni"); sBC3 = Sem("p3bc3")
        sH2 = Sem("p3h2"); sHPC = Sem("p3hpc"); sMX = Sem("p3mx"); sRS = Sem("p3rs")
        sBCr = Sem("p3bcr")
        sKA = Sem("p3ka"); sFO = Sem("p3fo"); sOD = Sem("p3od")

        for c in range(NCH3):
            if c >= 1:
                SY.wait_ge(sOP.h, FT * c)
            SY.dma_start(out=g16[:].rearrange("p (k t) -> p k t", k=16),
                         in_=bass.AP(cc_g_out, c * TC3,
                                     [[T, 128], [EH * T, 2], [128 * T, NDT], [1, TC3]])
                         ).then_inc(sGL.h, 16)
            sGL.n += 16
            if c >= 1:
                SY.wait_ge(sXLN3.h, 2 * c)
                SY.wait_ge(sSQ3.h, 2 * c)
                SY.wait_ge(sXFV3.h, c)
                SY.wait_ge(sXNW.h, FT * c)
            SY.dma_start(out=xtm3[:].rearrange("p (s d) -> p s d", s=2),
                         in_=bass.AP(x_in, c * TC3 * DM,
                                     [[DM, 128], [128 * DM, 2], [1, DM]])
                         ).then_inc(sXL3.h, 16)
            sXL3.n += 16

            V.wait_ge(sXL3.h, sXL3.n)
            A.wait_ge(sXL3.h, sXL3.n)
            for s in range(2):
                xs = xtm3[:, s * 1024:(s + 1) * 1024]
                i = V.tensor_reduce(st3[:, s:s + 1], xs, AXX, ALU.add)
                if s == 1:
                    sXFV3.inc(i)
                i = A.activation(scr3[:], xs, AF.Square, accum_out=st3[:, 2 + s:3 + s])
                sSQ3.inc(i)
            V.wait_ge(sSQ3.h, sSQ3.n)
            if c >= 1:
                V.wait_ge(sXLN3.h, 2 * c)
            m2 = st3[:, 4:6]; v2 = st3[:, 6:8]
            i = V.tensor_scalar_mul(m2, st3[:, 0:2], 1.0 / DM)
            vlock(i)
            i = V.tensor_scalar_mul(v2, st3[:, 2:4], 1.0 / DM)
            vlock(i)
            i = V.tensor_tensor(tiny[:, 0:2], m2, m2, ALU.mult)
            vlock(i)
            i = V.tensor_tensor(v2, v2, tiny[:, 0:2], ALU.subtract)
            vlock(i)
            i = V.tensor_scalar_add(v2, v2, EPS)
            vlock(i)
            inv2 = st3[:, 0:2]
            newton_rsqrt(inv2, v2, 128, 2)
            nm2 = st3[:, 2:4]
            i = V.tensor_scalar_mul(nm2, m2, -1.0)
            vlock(i)
            i = V.tensor_tensor(nm2, nm2, inv2, ALU.mult)
            sINV3.inc(i)
            A.wait_ge(sINV3.h, sINV3.n)
            for s in range(2):
                i = A.activation(xln3[:, s * 1024:(s + 1) * 1024],
                                 xtm3[:, s * 1024:(s + 1) * 1024], AF.Identity,
                                 bias=st3[:, 2 + s:3 + s], scale=st3[:, s:s + 1])
                sXLN3.inc(i)

            PE.wait_ge(sXLN3.h, sXLN3.n)
            for j in range(FT):
                bk = banks[j % 2]
                bk.begin(PE)
                for s in range(2):
                    i = PE.transpose(bk.t[:, s * 128:(s + 1) * 128],
                                     xln3[:, s * 1024 + j * 128:s * 1024 + (j + 1) * 128],
                                     identf[:])
                bk.produced(i)
                bk.wait_data(A)
                if c >= 1:
                    A.wait_ge(sAO.h, FT * c)
                i = A.activation(h3[:, j * TC3:(j + 1) * TC3], bk.t[:, 0:TC3], AF.Identity,
                                 bias=modp[:, 0 * FT + j:0 * FT + j + 1],
                                 scale=modp[:, 1 * FT + j:1 * FT + j + 1])
                sHEV3.inc(i)
                bk.set_free(sHEV3)

                obk = banks[2 + (j % 2)]
                obk.begin(PE)
                PE.wait_ge(sGL.h, sGL.n)
                if c == 0 and j == 0:
                    PE.wait_ge(sW3c.h, 16)
                for kt in range(16):
                    i = PE.matmul(obk.t[:, 0:TC3],
                                  wout16[:, kt * DM + j * 128:kt * DM + (j + 1) * 128],
                                  g16[:, kt * TC3:(kt + 1) * TC3],
                                  start=(kt == 0), stop=(kt == 15))
                sOP.inc(i)
                obk.set_data(sOP)
                xbk = banks[4 + (j % 2)]
                xbk.begin(PE)
                for s in range(2):
                    i = PE.transpose(xbk.t[:, s * 128:(s + 1) * 128],
                                     xtm3[:, s * 1024 + j * 128:s * 1024 + (j + 1) * 128],
                                     identf[:])
                xbk.produced(i)
                obk.wait_data(V)
                xbk.wait_data(V)
                V.wait_ge(sHEV3.h, sHEV3.n)
                i = V.tensor_tensor(mixt[:, 0:TC3], obk.t[:, 0:TC3],
                                    h3[:, j * TC3:(j + 1) * TC3], ALU.add)
                sAO.inc(i)
                obk.set_free(sAO)
                i = V.scalar_tensor_tensor(xnew[:, j * TC3:(j + 1) * TC3], mixt[:, 0:TC3],
                                           modp[:, 2 * FT + j:2 * FT + j + 1],
                                           xbk.t[:, 0:TC3], ALU.mult, ALU.add)
                sXNW.inc(i)
                xbk.set_free(sXNW)
                i = V.tensor_copy(xnew16[:, j * TC3:(j + 1) * TC3],
                                  xnew[:, j * TC3:(j + 1) * TC3])
                sXN16.inc(i)

            mbk = banks[6]
            mbk.begin(PE)
            for j in range(FT):
                PE.wait_ge(sXN16.h, sXN16.n - FT + j + 1)
                i = PE.matmul(mbk.t[0:1, 0:TC3], ones16[:],
                              xnew16[:, j * TC3:(j + 1) * TC3],
                              start=(j == 0), stop=(j == FT - 1))
            mbk.produced(i)
            qbk = banks[7]
            qbk.begin(PE)
            for j in range(FT):
                A.wait_ge(sXNW.h, sXNW.n - FT + j + 1)
                if j >= 2:
                    A.wait_ge(sSQF.h, sSQF.n - 1)
                i = A.activation(sq16[:], xnew[:, j * TC3:(j + 1) * TC3], AF.Square)
                sSQF.inc(i)
                PE.wait_ge(sSQF.h, sSQF.n)
                i = PE.matmul(qbk.t[0:1, 0:TC3], ones16[:], sq16[:],
                              start=(j == 0), stop=(j == FT - 1))
            qbk.produced(i)
            mbk.wait_data(V)
            qbk.wait_data(V)
            mrow = ln2s[0:1, 0:TC3]
            vrow = ln2s[0:1, TC3:2 * TC3]
            V.tensor_scalar_mul(mrow, mbk.t[0:1, 0:TC3], 1.0 / DM)
            V.tensor_scalar_mul(vrow, qbk.t[0:1, 0:TC3], 1.0 / DM)
            V.tensor_tensor(tiny2[0:1, 0:32], mrow[0:1, 0:32], mrow[0:1, 0:32], ALU.mult) \
                if False else None
            V.tensor_tensor(nr_f[0:1, 0:TC3], mrow, mrow, ALU.mult)
            V.tensor_tensor(vrow, vrow, nr_f[0:1, 0:TC3], ALU.subtract)
            V.tensor_scalar_add(vrow, vrow, EPS)
            newton_rsqrt(vrow, vrow, 1, TC3)
            i = V.tensor_copy(tiny[0:1, 48:49], vrow[0:1, 0:1])
            sLNI.inc(i)
            mbk.set_free(sLNI)
            qbk.set_free(sLNI)
            b1 = banks[6]
            b2 = banks[7]
            b1.begin(PE)
            b2.begin(PE)
            PE.wait_ge(sLNI.h, sLNI.n)
            i = PE.matmul(b1.t[:, 0:TC3], onesf_row[:], mrow, start=True, stop=True)
            b1.produced(i)
            i = PE.matmul(b2.t[:, 0:TC3], onesf_row[:], vrow, start=True, stop=True)
            b2.produced(i)
            b1.wait_data(V)
            b2.wait_data(V)

            if c >= 1:
                A.wait_ge(sMX.h, FT * c)
                V.wait_ge(sMX.h, FT * c)
            for j in range(FT):
                if c == 0:
                    i = V.memset(h2[:, j * (TC3 + 1):j * (TC3 + 1) + 1], 0.0)
                else:
                    i = V.tensor_copy(
                        h2[:, j * (TC3 + 1):j * (TC3 + 1) + 1],
                        h2[:, j * (TC3 + 1) + TC3:j * (TC3 + 1) + TC3 + 1])
                sHPC.inc(i)
                V.tensor_tensor(mixt[:, 0:TC3], xnew[:, j * TC3:(j + 1) * TC3],
                                b1.t[:, 0:TC3], ALU.subtract)
                i = V.tensor_tensor(mixt[:, 0:TC3], mixt[:, 0:TC3], b2.t[:, 0:TC3],
                                    ALU.mult)
                if j == FT - 1:
                    sBCr.inc(i)
                    b1.set_free(sBCr)
                    b2.set_free(sBCr)
                A.wait_ge(sHPC.h, sHPC.n)
                i = A.activation(h2[:, j * (TC3 + 1) + 1:(j + 1) * (TC3 + 1)],
                                 mixt[:, 0:TC3], AF.Identity,
                                 bias=modp[:, 3 * FT + j:3 * FT + j + 1],
                                 scale=modp[:, 4 * FT + j:4 * FT + j + 1])
                sH2.inc(i)
                V.wait_ge(sH2.h, sH2.n)

            for j in range(FT):
                h2c = h2[:, j * (TC3 + 1) + 1:(j + 1) * (TC3 + 1)]
                h2p = h2[:, j * (TC3 + 1):(j + 1) * (TC3 + 1) - 1]
                V.tensor_scalar_mul(mixt[:, 0:TC3], h2c, prm[:, 32 + j:33 + j])
                V.scalar_tensor_tensor(xr16[:, j * TC3:(j + 1) * TC3], h2p,
                                       prm[:, 48 + j:49 + j], mixt[:, 0:TC3],
                                       ALU.mult, ALU.add)
                V.tensor_scalar_mul(mixt[:, 0:TC3], h2c, prm[:, 40 + j:41 + j])
                i = V.scalar_tensor_tensor(xk16[:, j * TC3:(j + 1) * TC3], h2p,
                                           prm[:, 56 + j:57 + j], mixt[:, 0:TC3],
                                           ALU.mult, ALU.add)
                sMX.inc(i)

            for j in range(FT):
                rbk = banks[2 + (j % 2)]
                rbk.begin(PE)
                PE.wait_ge(sMX.h, sMX.n)
                for kt in range(FT):
                    i = PE.matmul(rbk.t[:, 0:TC3],
                                  wr16[:, kt * DM + j * 128:kt * DM + (j + 1) * 128],
                                  xr16[:, kt * TC3:(kt + 1) * TC3],
                                  start=(kt == 0), stop=(kt == FT - 1))
                rbk.produced(i)
                rbk.wait_data(A)
                i = A.activation(r16[:, j * TC3:(j + 1) * TC3], rbk.t[:, 0:TC3],
                                 AF.Sigmoid)
                sRS.inc(i)
                rbk.set_free(sRS)
            for j in range(FT):
                kbk = banks[4 + (j % 2)]
                kbk.begin(PE)
                for kt in range(FT):
                    i = PE.matmul(kbk.t[:, 0:TC3],
                                  wk16[:, kt * DM + j * 128:kt * DM + (j + 1) * 128],
                                  xk16[:, kt * TC3:(kt + 1) * TC3],
                                  start=(kt == 0), stop=(kt == FT - 1))
                kbk.produced(i)
                kbk.wait_data(A)
                i = A.activation(scr3[:, 0:TC3], kbk.t[:, 0:TC3], AF.Relu)
                i = A.activation(kact16[:, j * TC3:(j + 1) * TC3], scr3[:, 0:TC3],
                                 AF.Square)
                sKA.inc(i)
                kbk.set_free(sKA)
            for j in range(FT):
                vbk = banks[2 + (j % 2)]
                vbk.begin(PE)
                PE.wait_ge(sKA.h, sKA.n)
                for kt in range(FT):
                    i = PE.matmul(vbk.t[:, 0:TC3],
                                  wv16[:, kt * DM + j * 128:kt * DM + (j + 1) * 128],
                                  kact16[:, kt * TC3:(kt + 1) * TC3],
                                  start=(kt == 0), stop=(kt == FT - 1))
                vbk.produced(i)
                vbk.wait_data(V)
                V.wait_ge(sRS.h, sRS.n - FT + j + 1)
                i = V.tensor_tensor(mixt[:, 0:TC3], r16[:, j * TC3:(j + 1) * TC3],
                                    vbk.t[:, 0:TC3], ALU.mult)
                vbk.done(i)
                i = V.scalar_tensor_tensor(fout[:, j * TC3:(j + 1) * TC3], mixt[:, 0:TC3],
                                           modp[:, 5 * FT + j:5 * FT + j + 1],
                                           xnew[:, j * TC3:(j + 1) * TC3],
                                           ALU.mult, ALU.add)
                sFO.inc(i)

            for s in range(2):
                for hhalf in range(2):
                    tbk = banks[6 + hhalf]
                    tbk.begin(PE)
                    PE.wait_ge(sFO.h, sFO.n)
                    for jj in range(4):
                        j = hhalf * 4 + jj
                        i = PE.transpose(tbk.t[:, jj * 128:(jj + 1) * 128],
                                         h3[:, j * TC3 + s * 128:j * TC3 + s * 128 + 128]
                                         if False else
                                         fout[:, j * TC3 + s * 128:j * TC3 + s * 128 + 128],
                                         identf[:])
                    tbk.produced(i)
                    tbk.wait_data(A)
                    if c >= 1:
                        A.wait_ge(sOD.h, 16 * c)
                    i = A.activation(outtm[:, s * 1024 + hhalf * 512:
                                           s * 1024 + (hhalf + 1) * 512],
                                     tbk.t[:, 0:512], AF.Copy)
                    tbk.done(i)
            i = A.dma_start(out=bass.AP(out_t, c * TC3 * DM,
                                        [[DM, 128], [128 * DM, 2], [1, DM]]),
                            in_=outtm[:].rearrange("p (s d) -> p s d", s=2))
            sOD.inc(i, 16)

        if DEBUG_TAPS:
            sDBG = sDBGG
            for nm, srct in (("d_mod", modrow_d), ("d_xdbl_in", cc_xdbl_in),
                             ("d_xdbl_out", cc_xdbl_out), ("d_xi", spill_xi),
                             ("d_sr", spill_sr), ("d_gin", cc_g_in),
                             ("d_gout", cc_g_out)):
                i = SY.dma_start(out=dbg[nm].ap(), in_=srct.ap())
                sDBG.inc(i, 16)
            SY.wait_ge(sDBG.h, sDBG.n)
        SY.wait_ge(sOD.h, 16 * NCH3)
        nc.all_engine_barrier()
        SY.sem_clear(range(155, 256))
        nc.all_engine_barrier()
        ps.close()

    SY.sem_clear(range(155, 256))
    nc.all_engine_barrier()
    emit_prologue()
    p1 = emit_phase1()
    p2 = emit_phase2(p1)
    emit_phase3(p2)

    ges.close()
    return nc


def _percore_inputs(inputs, c):
    b, h = c // 2, c % 2
    f = np.float32
    x = np.ascontiguousarray(inputs["L0_attn_x"][b]).astype(f, copy=False)
    in_proj = inputs["L0_attn_in_proj_w"]
    w_in = np.ascontiguousarray(
        np.concatenate([in_proj[:, h * EH:(h + 1) * EH],
                        in_proj[:, 2048 + h * EH:2048 + (h + 1) * EH]], axis=1))
    conv_w = inputs["L0_attn_conv_w"][h * EH:(h + 1) * EH, 0, :]
    conv_diag = np.zeros((NDT * 4, 128, 128), f)
    for dt in range(NDT):
        for tap in range(4):
            np.fill_diagonal(conv_diag[dt * 4 + tap], conv_w[dt * 128:(dt + 1) * 128, tap])
    prm = np.zeros((128, 64), f)

    def col8(vec, base):
        prm[:, base:base + 8] = vec.reshape(8, 128).T

    col8(inputs["L0_attn_conv_b"][h * EH:(h + 1) * EH], 0)
    col8(inputs["L0_attn_dt_proj_b"][h * EH:(h + 1) * EH], 8)
    col8(inputs["L0_attn_D"][h * EH:(h + 1) * EH], 16)
    col8(inputs["L0_attn_norm_w"], 24)
    col8(inputs["L0_ffn_mu_r"], 32)
    col8(inputs["L0_ffn_mu_k"], 40)
    col8(1.0 - inputs["L0_ffn_mu_r"], 48)
    col8(1.0 - inputs["L0_ffn_mu_k"], 56)
    alog = inputs["L0_attn_A_log"][h * EH:(h + 1) * EH]
    a_log = np.zeros((128, 128), f)
    for dt in range(NDT):
        a_log[:, dt * NST:(dt + 1) * NST] = alog[dt * 128:(dt + 1) * 128, :]
    xpw = np.ascontiguousarray(inputs["L0_attn_x_proj_w"][h * EH:(h + 1) * EH, :])
    dt_w = np.ascontiguousarray(inputs["L0_attn_dt_proj_w"][:, h * EH:(h + 1) * EH])
    cvec = np.ascontiguousarray(inputs["L0_c"][b].reshape(8, 128).T)
    return {
        "x": x,
        "w_in": w_in.astype(f, copy=False),
        "conv_diag": conv_diag,
        "prm": prm,
        "a_log": a_log.astype(f, copy=False),
        "x_proj_w": xpw.astype(f, copy=False),
        "dt_w": dt_w.astype(f, copy=False),
        "w_out": np.ascontiguousarray(inputs["L0_attn_out_proj_w"]).astype(f, copy=False),
        "w_r": np.ascontiguousarray(inputs["L0_ffn_W_r"].T).astype(f, copy=False),
        "w_k": np.ascontiguousarray(inputs["L0_ffn_W_k"].T).astype(f, copy=False),
        "w_v": np.ascontiguousarray(inputs["L0_ffn_W_v"].T).astype(f, copy=False),
        "cvec": cvec,
        "wmod": np.ascontiguousarray(inputs["L0_wmod"]).astype(f, copy=False),
        "ident": np.eye(128, dtype=f),
        "magic": np.full((128, 512), MAGIC, dtype=np.int32),
    }


def kernel(**inputs):
    nc = build_program()
    in_maps = [_percore_inputs(inputs, c) for c in range(8)]
    res = run_bass_kernel_spmd(nc, in_maps, core_ids=list(range(8)))
    out = np.empty((B, T, DM), np.float32)
    for c in range(8):
        b, h = c // 2, c % 2
        out[b, h * TH:(h + 1) * TH, :] = res.results[c]["out"][h * TH:(h + 1) * TH]
    return out


# version 06 (final)
# speedup vs baseline: 1.0234x; 1.0234x over previous
import contextlib

import numpy as np
import ml_dtypes

import concourse.bass as bass
import concourse.mybir as mybir
from concourse.bass_utils import run_bass_kernel_spmd

F32 = mybir.dt.float32
BF16 = mybir.dt.bfloat16
I32 = mybir.dt.int32
AF = mybir.ActivationFunctionType
ALU = mybir.AluOpType
AXX = mybir.AxisListType.X

B, T, DM = 4, 8192, 1024
EH, NST, RK = 1024, 16, 64
FT = DM // 128
NDT = EH // 128
TC = 512
NCH1 = T // TC
OC = 2048
NOC = T // OC
TH = T // 2
TC3 = 256
NCH3 = T // TC3
POOL_N = 11
EPS = 1e-5
MAGIC = 0x5F3759DF
RG = [[0, 1], [2, 3], [4, 5], [6, 7]]
RACE_DETECT = False
DEBUG_TAPS = False
NEWTON_BYPASS = False


def build_program():
    nc = bass.Bass("TRN2", target_bir_lowering=False,
                   detect_race_conditions=RACE_DETECT)

    x_in = nc.dram_tensor("x", [T, DM], F32, kind="ExternalInput")
    w_in_ = nc.dram_tensor("w_in", [DM, 2 * EH], F32, kind="ExternalInput")
    convd_in = nc.dram_tensor("conv_diag", [NDT * 4, 128, 128], F32, kind="ExternalInput")
    prm_in = nc.dram_tensor("prm", [128, 64], F32, kind="ExternalInput")
    alog_in = nc.dram_tensor("a_log", [128, 128], F32, kind="ExternalInput")
    xpw_in = nc.dram_tensor("x_proj_w", [EH, 96], F32, kind="ExternalInput")
    dtw_in = nc.dram_tensor("dt_w", [RK, EH], F32, kind="ExternalInput")
    wout_in = nc.dram_tensor("w_out", [2 * EH, DM], F32, kind="ExternalInput")
    wr_in = nc.dram_tensor("w_r", [DM, DM], F32, kind="ExternalInput")
    wk_in = nc.dram_tensor("w_k", [DM, DM], F32, kind="ExternalInput")
    wv_in = nc.dram_tensor("w_v", [DM, DM], F32, kind="ExternalInput")
    cvec_in = nc.dram_tensor("cvec", [128, FT], F32, kind="ExternalInput")
    wmod_in = nc.dram_tensor("wmod", [DM, 6 * DM], F32, kind="ExternalInput")
    ident_in = nc.dram_tensor("ident", [128, 128], F32, kind="ExternalInput")
    magic_in = nc.dram_tensor("magic", [128, 512], I32, kind="ExternalInput")
    out_t = nc.dram_tensor("out", [T, DM], F32, kind="ExternalOutput")
    dbg = {}
    if DEBUG_TAPS:
        for nm, shape, dt_ in (("d_mod", [1, 6 * DM], F32),
                               ("d_xdbl_in", [96, T], F32),
                               ("d_xdbl_out", [96, T], F32),
                               ("d_xi", [EH, T], BF16),
                               ("d_sr", [EH, T], BF16),
                               ("d_gin", [EH, T], BF16),
                               ("d_gout", [2, EH, T], BF16),
                               ("d_mod_early", [1, 6 * DM], F32),
                               ("t_h", [128, FT * TC], F32),
                               ("t_xn", [128, FT * TC], BF16),
                               ("t_xz", [128, NDT * (TC + 3)], BF16),
                               ("t_stats", [128, 32], F32),
                               ("t_rmsi", [1, TC], F32),
                               ("t_xln", [128, 4 * 1024], F32),
                               ("t_magic", [128, 512], I32),
                               ("t_nri", [128, 512], I32)):
            dbg[nm] = nc.dram_tensor(nm, shape, dt_, kind="ExternalOutput")

    spill_xi = nc.dram_tensor("spill_xi", [EH, T], BF16, kind="Internal")
    spill_sr = nc.dram_tensor("spill_sr", [EH, T], BF16, kind="Internal")
    cc_xdbl_in = nc.dram_tensor("cc_xdbl_in", [96, T], F32, kind="Internal")
    cc_xdbl_out = nc.dram_tensor("cc_xdbl_out", [96, T], F32, kind="Internal")
    bc16buf = nc.dram_tensor("bc16buf", [NOC, 32, OC], BF16, kind="Internal")
    cc_g_in = nc.dram_tensor("cc_g_in", [EH, T], BF16, kind="Internal")
    cc_g_out = nc.dram_tensor("cc_g_out", [2, EH, T], BF16, kind="Internal")
    modrow_d = nc.dram_tensor("modrow_d", [1, 6 * DM], F32, kind="Internal")

    ges = contextlib.ExitStack()

    def mk_sb(stk):
        def sb(name, shape, dt):
            return stk.enter_context(nc.sbuf_tensor(name, shape, dt))
        return sb

    gsb = mk_sb(ges)

    class Sem:
        __slots__ = ("h", "n")

        def __init__(self, name):
            self.h = ges.enter_context(nc.semaphore(name))
            self.n = 0

        def inc(self, instr, k=1):
            instr.then_inc(self.h, k)
            self.n += k
            return self.n

    class Bank:

        def __init__(self, name):
            self.t = ges.enter_context(nc.psum_tensor(name, [128, 512], F32))
            self.sp = Sem(name + "p")
            self.sf = Sem(name + "f")
            self.data = None
            self.free = None

        def begin(self, eng):
            if self.free is not None:
                eng.wait_ge(self.free[0], self.free[1])

        def produced(self, instr):
            self.sp.inc(instr)
            self.data = (self.sp.h, self.sp.n)

        def set_data(self, sem):
            self.data = (sem.h, sem.n)

        def wait_data(self, eng):
            eng.wait_ge(self.data[0], self.data[1])

        def done(self, instr):
            self.sf.inc(instr)
            self.free = (self.sf.h, self.sf.n)

        def set_free(self, sem):
            self.free = (sem.h, sem.n)

    V, A, PE, PL, SY = nc.vector, nc.scalar, nc.tensor, nc.gpsimd, nc.sync

    dtw16 = gsb("dtw16", [RK, EH], BF16)
    xpw16 = gsb("xpw16", [128, FT * 96], BF16)
    prm = gsb("prm_s", [128, 64], F32)
    magic = gsb("magic_s", [128, 512], I32)
    identf = gsb("identf", [128, 128], F32)
    ident16 = gsb("ident16", [128, 128], BF16)
    ones16 = gsb("ones16", [128, 1], BF16)
    onesf_row = gsb("onesf_row", [1, 128], F32)
    asc = gsb("asc", [128, 128], F32)
    state = gsb("state", [128, NDT * NST], F32)
    modp = gsb("modp", [128, 6 * FT], F32)
    tiny = gsb("tiny", [128, 64], F32)
    tiny2 = gsb("tiny2", [128, 64], F32)
    nr_i = gsb("nr_i", [128, 512], I32)
    nr_f = gsb("nr_f", [128, 512], F32)

    banks = [Bank(f"bk{i}") for i in range(8)]

    sPRO = Sem("sPRO")
    sCC = Sem("sCC")
    sA2A = Sem("sA2A")

    sVL = Sem("sVL")
    sDBGG = Sem("sDBGG") if DEBUG_TAPS else None

    def vlock(instr):
        sVL.inc(instr)
        V.wait_ge(sVL.h, sVL.n)

    def newton_rsqrt(dst, src, p, n, it=3):
        if NEWTON_BYPASS:
            V.tensor_single_scalar(dst, src, 1.0, ALU.min)
            return
        tiny_op = n < 128
        bits = nr_i[0:p, 0:n]
        i = V.tensor_single_scalar(bits, src.bitcast(I32), 1, ALU.arith_shift_right)
        if tiny_op:
            vlock(i)
        i = V.tensor_tensor(bits, magic[0:p, 0:n], bits, ALU.subtract)
        if tiny_op:
            vlock(i)
        i = V.tensor_copy(dst, bits.bitcast(F32))
        if tiny_op:
            vlock(i)
        for _ in range(it):
            t = nr_f[0:p, 0:n]
            i = V.tensor_tensor(t, dst, dst, ALU.mult)
            if tiny_op:
                vlock(i)
            i = V.tensor_tensor(t, t, src, ALU.mult)
            if tiny_op:
                vlock(i)
            i = V.tensor_scalar(t, t, -0.5, 1.5, ALU.mult, ALU.add)
            if tiny_op:
                vlock(i)
            i = V.tensor_tensor(dst, dst, t, ALU.mult)
            if tiny_op:
                vlock(i)

    def emit_prologue():
        ps = contextlib.ExitStack()
        sb = mk_sb(ps)
        sLD = Sem("sLD")
        SY.dma_start(out=prm[:], in_=prm_in.ap()).then_inc(sLD.h, 16)
        SY.dma_start(out=magic[:], in_=magic_in.ap()).then_inc(sLD.h, 16)
        SY.dma_start(out=identf[:], in_=ident_in.ap()).then_inc(sLD.h, 16)
        SY.dma_start(out=tiny[:, 0:FT], in_=cvec_in.ap()).then_inc(sLD.h, 16)
        dtw_f = sb("dtw_f", [RK, EH], F32)
        SY.dma_start(out=dtw_f[:], in_=dtw_in.ap()).then_inc(sLD.h, 16)
        xpw_f = sb("xpw_f", [128, FT * 96], F32)
        SY.dma_start(out=xpw_f[:].rearrange("p (k n) -> p k n", k=FT),
                     in_=bass.AP(xpw_in, 0, [[96, 128], [96 * 128, FT], [1, 96]])
                     ).then_inc(sLD.h, 16)
        sLD.n = 6 * 16
        A.wait_ge(sLD.h, sLD.n)
        A.copy(dtw16[:], dtw_f[:])
        A.copy(xpw16[:], xpw_f[:])
        A.copy(ident16[:], identf[:])
        i = A.activation(tiny2[:, 0:FT], tiny[:, 0:FT], AF.Silu)
        sSIL = Sem("sSIL")
        sSIL.inc(i)
        sPRO.inc(i) if False else None
        V.memset(ones16[:], 1.0)
        i = V.memset(onesf_row[:], 1.0)
        sPRO.inc(i)
        i = V.memset(state[:], 0.0)
        sPRO.inc(i)

        siluc = sb("siluc", [128, FT], BF16)
        V.wait_ge(sSIL.h, 1)
        i = V.tensor_copy(siluc[:], tiny2[:, 0:FT])
        sPRO.inc(i)

        wst = sb("wst", [128, 1536], F32)
        wst16 = sb("wst16", [128, 1536], BF16)
        modrow = sb("modrow", [1, 6 * DM], F32)
        sWM = Sem("sWM")
        sWC = Sem("sWC")
        sMM = Sem("sMM")
        sWMM = Sem("sWMM")
        for g in range(4):
            for kt in range(FT):
                step = g * FT + kt
                if step >= 1:
                    SY.wait_ge(sWC.h, step)
                SY.dma_start(out=wst[:],
                             in_=bass.AP(wmod_in, kt * 6 * DM * 128 + g * 1536,
                                         [[6 * DM, 128], [1, 1536]])
                             ).then_inc(sWM.h, 16)
                sWM.n += 16
                A.wait_ge(sWM.h, sWM.n)
                if kt == 0:
                    if g >= 1:
                        A.wait_ge(sMM.h, 3 * g)
                else:
                    A.wait_ge(sWMM.h, 7 * g + kt)
                i = A.copy(wst16[:], wst[:])
                sWC.inc(i)
                PE.wait_ge(sWC.h, sWC.n)
                if step == 0:
                    PE.wait_ge(sPRO.h, 3)
                for s3 in range(3):
                    bk = banks[s3]
                    if kt == 0:
                        bk.begin(PE)
                    i = PE.matmul(bk.t[0:1, 0:512], siluc[:, kt:kt + 1],
                                  wst16[:, s3 * 512:(s3 + 1) * 512],
                                  start=(kt == 0), stop=(kt == FT - 1))
                    if kt == FT - 1:
                        bk.produced(i)
                        bk.wait_data(V)
                        i = V.tensor_copy(
                            modrow[0:1, g * 1536 + s3 * 512:g * 1536 + (s3 + 1) * 512],
                            bk.t[0:1, 0:512])
                        sMM.inc(i)
                        bk.set_free(sMM)
                    elif s3 == 2:
                        sWMM.inc(i)
        sMR = Sem("sMR")
        SY.wait_ge(sMM.h, sMM.n)
        SY.dma_start(out=modrow_d.ap(), in_=modrow[:]).then_inc(sMR.h, 16)
        SY.wait_ge(sMR.h, 16)
        SY.dma_start(out=modp[:].rearrange("p (j f) -> p j f", j=6),
                     in_=bass.AP(modrow_d, 0, [[1, 128], [DM, 6], [128, FT]]),
                     allow_slow_non_contiguous=True).then_inc(sMR.h, 16)
        V.wait_ge(sMR.h, 32)
        V.tensor_scalar_add(modp[:, 1 * FT:2 * FT], modp[:, 1 * FT:2 * FT], 1.0)
        V.tensor_scalar_add(modp[:, 4 * FT:5 * FT], modp[:, 4 * FT:5 * FT], 1.0)
        sGAB = Sem("sGAB")
        A.wait_ge(sMR.h, 32)
        for ci, col in enumerate((2, 5)):
            sl = modp[:, col * FT:(col + 1) * FT]
            ab = tiny[:, 8 + ci * FT:8 + (ci + 1) * FT]
            sg = tiny2[:, 8 + ci * FT:8 + (ci + 1) * FT]
            i = A.activation(ab, sl, AF.Abs)
            i = A.activation(sg, sl, AF.Sign)
            sGAB.inc(i)
            V.wait_ge(sGAB.h, sGAB.n)
            i = V.tensor_scalar_max(ab, ab, 0.25)
            vlock(i)
            i = V.tensor_tensor(sl, ab, sg, ALU.mult)
        sPRO.inc(i)
        if DEBUG_TAPS:
            sMT = sDBGG
            SY.wait_ge(sMR.h, 32)
            i = SY.dma_start(out=dbg["d_mod_early"].ap(), in_=modrow_d.ap())
            sMT.inc(i, 16)
            SY.wait_ge(sMT.h, 16)
        SY.wait_ge(sMR.h, 32)
        SY.wait_ge(sPRO.h, 4)
        SY.wait_ge(sGAB.h, sGAB.n)
        nc.all_engine_barrier()
        ps.close()

    def emit_phase1():
        ps = contextlib.ExitStack()
        sb = mk_sb(ps)
        win16 = sb("win16", [128, FT * 2048], BF16)
        convd16 = sb("convd16", [128, NDT * 512], BF16)
        xtm = [sb(f"xtm{i}", [128, 4 * 1024], F32) for i in range(2)]
        xln = sb("xln", [128, 4 * 1024], F32)
        hfm = sb("hfm", [128, FT * TC], F32)
        xn16 = sb("xn16", [128, FT * TC], BF16)
        xz16 = sb("xz16", [128, NDT * (TC + 3)], BF16)
        xi16c = [sb(f"xi16c{i}", [128, NDT * TC], BF16) for i in range(2)]
        sr16c = [sb(f"sr16c{i}", [128, NDT * TC], BF16) for i in range(2)]
        hh16 = [sb(f"hh16{i}", [128, TC], BF16) for i in range(2)]
        xdst = [sb(f"xdst{i}", [96, TC], F32) for i in range(2)]
        scr = sb("scr1", [128, 1024], F32)
        stats = sb("stats1", [128, 32], F32)
        rmsi = sb("rmsi", [1, TC], F32)
        wstage = sb("wstage1", [128, 2048], F32)
        cstage = sb("cstage", [128, 512], F32)

        sWS = Sem("sWS"); sWL = Sem("sWL")
        sXL = Sem("p1xl"); sXFV = Sem("p1xfv")
        sSQ = Sem("p1sq"); sINV = Sem("p1inv"); sXLN = Sem("p1xln")
        sHEV = Sem("p1hev"); sHH = Sem("p1hh"); sONE = Sem("p1one")
        sRM = Sem("p1rm"); sXN = Sem("p1xn")
        sXZ = Sem("p1xz"); sHC = Sem("p1hc"); sCV = Sem("p1cv")
        sXI = Sem("p1xi"); sSR = Sem("p1sr")
        sXID = Sem("p1xid"); sSRD = Sem("p1srd")
        sXPD = Sem("p1xpd"); sXDE = Sem("p1xde"); sXDD = Sem("p1xdd")

        for kt in range(FT):
            if kt >= 1:
                SY.wait_ge(sWL.h, kt)
            SY.dma_start(out=wstage[:],
                         in_=bass.AP(w_in_, kt * 128 * 2 * EH, [[2 * EH, 128], [1, 2048]])
                         ).then_inc(sWS.h, 16)
            sWS.n += 16
            A.wait_ge(sWS.h, sWS.n)
            i = A.copy(win16[:, kt * 2048:(kt + 1) * 2048], wstage[:])
            sWL.inc(i)
        for dt in range(NDT):
            SY.wait_ge(sWL.h, FT + dt)
            SY.dma_start(out=cstage[:].rearrange("p (t n) -> p t n", t=4),
                         in_=bass.AP(convd_in, dt * 4 * 128 * 128,
                                     [[128, 128], [128 * 128, 4], [1, 128]])
                         ).then_inc(sWS.h, 16)
            sWS.n += 16
            A.wait_ge(sWS.h, sWS.n)
            i = A.copy(convd16[:, dt * 512:(dt + 1) * 512], cstage[:])
            sWL.inc(i)

        for dt in range(NDT):
            i = V.memset(xz16[:, dt * (TC + 3):dt * (TC + 3) + 3], 0.0)
            sHC.inc(i)

        for c in range(NCH1):
            p = c % 2
            so = 16 * p
            if c >= 2:
                SY.wait_ge(sXLN.h, 4 * (c - 1))
                SY.wait_ge(sSQ.h, 4 * (c - 1))
                SY.wait_ge(sXFV.h, c - 1)
            SY.dma_start(out=xtm[p][:].rearrange("p (s d) -> p s d", s=4),
                         in_=bass.AP(x_in, c * TC * DM,
                                     [[DM, 128], [128 * DM, 4], [1, DM]])
                         ).then_inc(sXL.h, 16)
            sXL.n += 16

            V.wait_ge(sXL.h, sXL.n)
            A.wait_ge(sXL.h, sXL.n)
            for s in range(4):
                xs = xtm[p][:, s * 1024:(s + 1) * 1024]
                i = V.tensor_reduce(stats[:, so + s:so + s + 1], xs, AXX, ALU.add)
                if s == 3:
                    sXFV.inc(i)
                i = A.activation(scr[:], xs, AF.Square,
                                 accum_out=stats[:, so + 4 + s:so + 5 + s])
                sSQ.inc(i)
            V.wait_ge(sSQ.h, sSQ.n)
            if c >= 1:
                V.wait_ge(sXLN.h, 4 * c)
            m4 = stats[:, so + 8:so + 12]
            v4 = stats[:, so + 12:so + 16]
            i = V.tensor_scalar_mul(m4, stats[:, so:so + 4], 1.0 / DM)
            vlock(i)
            i = V.tensor_scalar_mul(v4, stats[:, so + 4:so + 8], 1.0 / DM)
            vlock(i)
            i = V.tensor_tensor(tiny[:, 0:4], m4, m4, ALU.mult)
            vlock(i)
            i = V.tensor_tensor(v4, v4, tiny[:, 0:4], ALU.subtract)
            vlock(i)
            i = V.tensor_scalar_add(v4, v4, EPS)
            vlock(i)
            inv4 = stats[:, so:so + 4]
            newton_rsqrt(inv4, v4, 128, 4)
            nm4 = stats[:, so + 4:so + 8]
            i = V.tensor_scalar_mul(nm4, m4, -1.0)
            vlock(i)
            i = V.tensor_tensor(nm4, nm4, inv4, ALU.mult)
            sINV.inc(i)

            A.wait_ge(sINV.h, sINV.n)
            if c >= 1:
                A.wait_ge(sHEV.h, FT * c)
            for s in range(4):
                i = A.activation(xln[:, s * 1024:(s + 1) * 1024],
                                 xtm[p][:, s * 1024:(s + 1) * 1024], AF.Identity,
                                 bias=stats[:, so + 4 + s:so + 5 + s],
                                 scale=stats[:, so + s:so + s + 1])
                sXLN.inc(i)

            PE.wait_ge(sXLN.h, sXLN.n)
            if c == 0:
                PE.wait_ge(sPRO.h, 4)
            for j in range(FT):
                bk = banks[j % 2]
                bk.begin(PE)
                for s in range(4):
                    i = PE.transpose(bk.t[:, s * 128:(s + 1) * 128],
                                     xln[:, s * 1024 + j * 128:s * 1024 + (j + 1) * 128],
                                     identf[:])
                bk.produced(i)
                bk.wait_data(A)
                if c >= 1:
                    A.wait_ge(sXN.h, FT * c)
                i = A.activation(hfm[:, j * TC:(j + 1) * TC], bk.t[:], AF.Identity,
                                 bias=modp[:, 0 * FT + j:0 * FT + j + 1],
                                 scale=modp[:, 1 * FT + j:1 * FT + j + 1])
                sHEV.inc(i)
                bk.set_free(sHEV)

            rbk = banks[7]
            rbk.begin(PE)
            for j in range(FT):
                A.wait_ge(sHEV.h, sHEV.n - FT + j + 1)
                if 8 * c + j >= 2:
                    A.wait_ge(sONE.h, 8 * c + j - 1)
                i = A.activation(hh16[j % 2][:], hfm[:, j * TC:(j + 1) * TC], AF.Square)
                sHH.inc(i)
                PE.wait_ge(sHH.h, sHH.n)
                i = PE.matmul(rbk.t[0:1, 0:TC], ones16[:], hh16[j % 2][:],
                              start=(j == 0), stop=(j == FT - 1))
                sONE.inc(i)
            rbk.set_data(sONE)
            rbk.wait_data(V)
            V.tensor_scalar(rmsi[0:1, :], rbk.t[0:1, 0:TC], 1.0 / DM, EPS,
                            ALU.mult, ALU.add)
            newton_rsqrt(rmsi[0:1, :], rmsi[0:1, :], 1, TC)
            i = V.tensor_copy(tiny[0:1, 32:33], rmsi[0:1, 0:1])
            sRM.inc(i)
            rbk.set_free(sRM)
            bbk = banks[6]
            bbk.begin(PE)
            PE.wait_ge(sRM.h, sRM.n)
            i = PE.matmul(bbk.t[:, 0:TC], onesf_row[:], rmsi[0:1, :],
                          start=True, stop=True)
            bbk.produced(i)
            bbk.wait_data(V)
            if c >= 1:
                V.wait_ge(sXZ.h, NDT * c)
                V.wait_ge(sSR.h, NDT * c)
            for j in range(FT):
                i = V.scalar_tensor_tensor(xn16[:, j * TC:(j + 1) * TC],
                                           hfm[:, j * TC:(j + 1) * TC],
                                           prm[:, 24 + j:25 + j],
                                           bbk.t[:, 0:TC], ALU.mult, ALU.mult)
                sXN.inc(i)
            bbk.set_free(sXN)

            PE.wait_ge(sXN.h, sXN.n)
            if c == 0:
                PE.wait_ge(sWL.h, FT + NDT)
            for m in range(16):
                bk = banks[2 + (m % 2)]
                bk.begin(PE)
                for kt in range(FT):
                    i = PE.matmul(bk.t[:, 0:TC],
                                  win16[:, kt * 2048 + m * 128:kt * 2048 + (m + 1) * 128],
                                  xn16[:, kt * TC:(kt + 1) * TC],
                                  start=(kt == 0), stop=(kt == FT - 1))
                bk.produced(i)
                bk.wait_data(A)
                if m < NDT:
                    dt = m
                    A.wait_ge(sHC.h, 8 * (c + 1))
                    i = A.activation(xz16[:, dt * (TC + 3) + 3:(dt + 1) * (TC + 3)],
                                     bk.t[:, 0:TC], AF.Copy)
                    sXZ.inc(i)
                    bk.set_free(sXZ)
                else:
                    dt = m - NDT
                    if c >= 2:
                        A.wait_ge(sSRD.h, 16 * (c - 1))
                    i = A.activation(sr16c[p][:, dt * TC:(dt + 1) * TC],
                                     bk.t[:, 0:TC], AF.Silu)
                    sSR.inc(i)
                    bk.set_free(sSR)

            for dt in range(NDT):
                bk = banks[4 + (dt % 2)]
                bk.begin(PE)
                PE.wait_ge(sXZ.h, sXZ.n - NDT + dt + 1)
                for tap in range(4):
                    i = PE.matmul(bk.t[:, 0:TC],
                                  convd16[:, dt * 512 + tap * 128:dt * 512 + (tap + 1) * 128],
                                  xz16[:, dt * (TC + 3) + tap:dt * (TC + 3) + tap + TC],
                                  start=(tap == 0), stop=(tap == 3))
                sCV.inc(i)
                bk.set_data(sCV)
                bk.wait_data(A)
                if c >= 2:
                    A.wait_ge(sXID.h, 16 * (c - 1))
                i = A.activation(xi16c[p][:, dt * TC:(dt + 1) * TC], bk.t[:, 0:TC],
                                 AF.Silu, bias=prm[:, 0 + dt:1 + dt])
                sXI.inc(i)
                bk.set_free(sXI)
                V.wait_ge(sCV.h, sCV.n)
                i = V.tensor_copy(xz16[:, dt * (TC + 3):dt * (TC + 3) + 3],
                                  xz16[:, dt * (TC + 3) + TC:dt * (TC + 3) + TC + 3])
                sHC.inc(i)

            xbk = banks[2 + (c % 2)]
            xbk.begin(PE)
            PE.wait_ge(sXI.h, sXI.n)
            for kt in range(NDT):
                i = PE.matmul(xbk.t[0:96, 0:TC], xpw16[:, kt * 96:(kt + 1) * 96],
                              xi16c[p][:, kt * TC:(kt + 1) * TC],
                              start=(kt == 0), stop=(kt == NDT - 1))
            sXPD.inc(i)
            xbk.set_data(sXPD)
            xbk.wait_data(V)
            if c >= 2:
                V.wait_ge(sXDD.h, 16 * (c - 1))
            i = V.tensor_copy(xdst[p][0:96, :], xbk.t[0:96, 0:TC])
            sXDE.inc(i)
            xbk.set_free(sXDE)

            A.wait_ge(sXI.h, sXI.n)
            i = A.dma_start(out=bass.AP(spill_xi, c * TC,
                                        [[T, 128], [128 * T, NDT], [1, TC]]),
                            in_=xi16c[p][:].rearrange("p (d t) -> p d t", d=NDT))
            sXID.inc(i, 16)
            A.wait_ge(sSR.h, sSR.n)
            i = A.dma_start(out=bass.AP(spill_sr, c * TC,
                                        [[T, 128], [128 * T, NDT], [1, TC]]),
                            in_=sr16c[p][:].rearrange("p (d t) -> p d t", d=NDT))
            sSRD.inc(i, 16)
            SY.wait_ge(sXDE.h, sXDE.n)
            i = SY.dma_start(out=bass.AP(cc_xdbl_in, c * TC, [[T, 96], [1, TC]]),
                             in_=xdst[p][0:96, :])
            sXDD.inc(i, 16)
            if DEBUG_TAPS and c == 0:
                sT1 = sDBGG
                SY.wait_ge(sXI.h, NDT)
                SY.wait_ge(sSR.h, NDT)
                for nm, srcap in (("t_h", hfm[:]), ("t_xn", xn16[:]),
                                  ("t_xz", xz16[:]), ("t_stats", stats[:]),
                                  ("t_rmsi", rmsi[:]), ("t_xln", xln[:]),
                                  ("t_magic", magic[:]), ("t_nri", nr_i[:])):
                    i = SY.dma_start(out=dbg[nm].ap(), in_=srcap)
                    sT1.inc(i, 16)
                SY.wait_ge(sT1.h, sT1.n)

        PL.wait_ge(sXDD.h, 16 * NCH1)
        i = PL.collective_compute("AllReduce", ALU.add, replica_groups=RG,
                                  ins=[cc_xdbl_in.ap()], outs=[cc_xdbl_out.ap()])
        sCC.inc(i)
        ps.close()
        return dict(sXID=sXID, sSRD=sSRD, sXDD=sXDD, sXDE=sXDE, sXPD=sXPD,
                    sXI=sXI, sSR=sSR)

    def emit_phase2(p1):
        for eng in (SY, A, V, PE, PL):
            eng.wait_ge(sCC.h, 1)
            eng.wait_ge(p1["sXID"].h, p1["sXID"].n)
            eng.wait_ge(p1["sSRD"].h, p1["sSRD"].n)
            eng.wait_ge(p1["sXDD"].h, p1["sXDD"].n)
            eng.wait_ge(p1["sXDE"].h, p1["sXDE"].n)
            eng.wait_ge(p1["sXPD"].h, p1["sXPD"].n)

        ps = contextlib.ExitStack()
        sb = mk_sb(ps)
        delta = [sb(f"delta{i}", [128, OC], F32) for i in range(2)]
        du16 = [sb(f"du16{i}", [128, OC], BF16) for i in range(2)]
        xi16s = [sb(f"xi16s{i}", [128, OC], BF16) for i in range(2)]
        sr16s = [sb(f"sr16s{i}", [128, OC], BF16) for i in range(2)]
        dab = [sb(f"dab{i}", [128, OC], F32) for i in range(3)]
        dbub = [sb(f"dbub{i}", [128, OC], BF16) for i in range(3)]
        hb = [sb(f"hb{i}", [128, OC], BF16) for i in range(3)]
        z16b = [sb(f"z16b{i}", [128, OC], BF16) for i in range(3)]
        bbc = [sb(f"bbc{i}", [128, OC], BF16) for i in range(3)]
        cbc = [sb(f"cbc{i}", [128, OC], BF16) for i in range(3)]
        g16b = [sb(f"g16b{i}", [128, OC], BF16) for i in range(2)]
        xdblk = sb("xdblk", [RK, OC], F32)
        xdblk16 = sb("xdblk16", [RK, OC], BF16)
        bcst = sb("bcst", [32, OC], F32)
        bcst16 = sb("bcst16", [32, OC], BF16)
        dpre = sb("dpre", [128, OC], F32)
        aabs = sb("aabs", [128, OC], F32)
        sexp = sb("sexp", [128, OC], F32)
        sln = sb("sln", [128, OC], F32)
        gt1 = sb("gt1", [128, 512], F32)
        alog2 = sb("alog2", [128, 128], F32)

        sASC = Sem("p2asc")
        sXDL = Sem("p2xdl"); sBCL = Sem("p2bcl"); sBCC = Sem("p2bcc"); sBCD = Sem("p2bcd")
        sXDK = Sem("p2xdk"); sDTP = Sem("p2dtp"); sSPE = Sem("p2spe"); sSPL = Sem("p2spl")
        sDEL = Sem("p2del"); sXI2 = Sem("p2xi2"); sSR2 = Sem("p2sr2"); sDU = Sem("p2du")
        sBB = Sem("p2bb"); sCB = Sem("p2cb"); sDA = Sem("p2da")
        sDBv = Sem("p2dbv"); sDBp = Sem("p2dbp"); sZv = Sem("p2zv"); sZp = Sem("p2zp")
        dbu_rec = {}; z_rec = {}
        sSC = Sem("p2sc"); sY = Sem("p2y"); sGT = Sem("p2gt")
        sGD = Sem("p2gd")

        SY.dma_start(out=alog2[:], in_=alog_in.ap()).then_inc(sASC.h, 16)
        sASC.n += 16
        A.wait_ge(sASC.h, sASC.n)
        i = A.activation(asc[:], alog2[:], AF.Exp)
        sASC.inc(i)
        V.wait_ge(sASC.h, sASC.n)
        i = V.tensor_scalar_mul(asc[:], asc[:], -1.0)
        sASC.inc(i)

        NIT = NOC * NDT * NST
        pool_n = lambda nn: (nn % 3) != 2

        def stage_bc(oc2):
            if oc2 >= 1:
                SY.wait_ge(sBCD.h, 16 * oc2)
            SY.dma_start(out=bcst[:],
                         in_=bass.AP(cc_xdbl_out, RK * T + oc2 * OC, [[T, 32], [1, OC]])
                         ).then_inc(sBCL.h, 16)
            sBCL.n += 16
            V.wait_ge(sBCL.h, sBCL.n)
            if oc2 >= 1:
                V.wait_ge(sBCD.h, 16 * oc2)
            i = V.tensor_copy(bcst16[:], bcst[:])
            sBCC.inc(i)
            SY.wait_ge(sBCC.h, sBCC.n)
            i = SY.dma_start(out=bass.AP(bc16buf, oc2 * 32 * OC, [[OC, 32], [1, OC]]),
                             in_=bcst16[:])
            sBCD.inc(i, 16)

        bbc_cnt = {}
        bbc_oc = [-1]

        def fetch_bbc(it2):
            if it2 >= NIT:
                return
            oc2, r = divmod(it2, NDT * NST)
            n2 = r % NST
            if it2 >= 3:
                SY.wait_ge(*dbu_rec[it2 - 3])
            if oc2 != bbc_oc[0]:
                SY.wait_ge(sBCD.h, 16 * (oc2 + 1))
                bbc_oc[0] = oc2
            SY.dma_start(out=bbc[it2 % 3][:],
                         in_=bass.AP(bc16buf, (oc2 * 32 + n2) * OC,
                                     [[0, 128], [1, OC]])
                         ).then_inc(sBB.h, 16)
            sBB.n += 16
            bbc_cnt[it2] = sBB.n

        def emit_dbu(it2):
            if it2 >= NIT:
                return
            oc2, r = divmod(it2, NDT * NST)
            dt2, n2 = divmod(r, NST)
            pp2 = dt2 % 2
            eng = PL if pool_n(n2) else V
            sdb = sDBp if pool_n(n2) else sDBv
            eng.wait_ge(sDU.h, oc2 * NDT + dt2 + 1)
            eng.wait_ge(sBB.h, bbc_cnt[it2])
            if it2 >= 3:
                eng.wait_ge(sSC.h, it2 - 2)
            i = eng.tensor_tensor(dbub[it2 % 3][:], du16[pp2][:], bbc[it2 % 3][:],
                                  ALU.mult)
            sdb.inc(i)
            dbu_rec[it2] = (sdb.h, sdb.n)

        stage_bc(0)
        fetch_bbc(0)
        fetch_bbc(1)
        it = 0
        for oc in range(NOC):
            if oc + 1 < NOC:
                stage_bc(oc + 1)
            if oc >= 1:
                SY.wait_ge(sDTP.h, 4 * NDT * oc)
            SY.dma_start(out=xdblk[:],
                         in_=bass.AP(cc_xdbl_out, oc * OC, [[T, RK], [1, OC]])
                         ).then_inc(sXDL.h, 16)
            sXDL.n += 16
            V.wait_ge(sXDL.h, sXDL.n)
            if oc >= 1:
                V.wait_ge(sDTP.h, 4 * NDT * oc)
            i = V.tensor_copy(xdblk16[:], xdblk[:])
            sXDK.inc(i)

            for dt in range(NDT):
                dtg = oc * NDT + dt
                pp = dt % 2
                bset = banks[0:4] if pp == 0 else banks[4:8]
                PE.wait_ge(sXDK.h, oc + 1)
                for sub in range(4):
                    bk = bset[sub]
                    bk.begin(PE)
                    i = PE.matmul(bk.t[:, 0:512], dtw16[:, dt * 128:(dt + 1) * 128],
                                  xdblk16[:, sub * 512:(sub + 1) * 512],
                                  start=True, stop=True)
                    sDTP.inc(i)
                    bk.set_data(sDTP)
                if dtg >= 1:
                    A.wait_ge(sDEL.h, dtg)
                for sub in range(4):
                    bk = bset[sub]
                    bk.wait_data(A)
                    i = A.activation(aabs[:, sub * 512:(sub + 1) * 512], bk.t[:, 0:512],
                                     AF.Abs, bias=prm[:, 8 + dt:9 + dt])
                    i = A.activation(dpre[:, sub * 512:(sub + 1) * 512], bk.t[:, 0:512],
                                     AF.Identity, bias=prm[:, 8 + dt:9 + dt])
                    sSPE.inc(i)
                    bk.set_free(sSPE)
                i = A.activation(sexp[:], aabs[:], AF.Exp, scale=-1.0)
                i = A.activation(sln[:], sexp[:], AF.Ln, bias=1.0)
                sSPL.inc(i)
                V.wait_ge(sSPL.h, sSPL.n)
                if dtg >= 2:
                    V.wait_ge(sDA.h, (dtg - 1) * NST)
                i = V.scalar_tensor_tensor(delta[pp][:], dpre[:], 0.0, sln[:],
                                           ALU.max, ALU.add)
                sDEL.inc(i)
                if dtg >= 2:
                    SY.wait_ge(sDU.h, dtg - 1)
                    SY.wait_ge(sGT.h, 4 * (dtg - 1))
                SY.dma_start(out=xi16s[pp][:],
                             in_=bass.AP(spill_xi, dt * 128 * T + oc * OC,
                                         [[T, 128], [1, OC]])
                             ).then_inc(sXI2.h, 16)
                sXI2.n += 16
                SY.dma_start(out=sr16s[pp][:],
                             in_=bass.AP(spill_sr, dt * 128 * T + oc * OC,
                                         [[T, 128], [1, OC]])
                             ).then_inc(sSR2.h, 16)
                sSR2.n += 16
                V.wait_ge(sXI2.h, sXI2.n)
                i = V.tensor_tensor(du16[pp][:], delta[pp][:], xi16s[pp][:], ALU.mult)
                sDU.inc(i)
                emit_dbu(it)

                for n in range(NST):
                    np_ = it % 3
                    fetch_bbc(it + 2)
                    if n < NST - 1:
                        emit_dbu(it + 1)
                    if it >= 3:
                        SY.wait_ge(*z_rec[it - 3])
                    SY.dma_start(out=cbc[np_][:],
                                 in_=bass.AP(bc16buf, (oc * 32 + 16 + n) * OC,
                                             [[0, 128], [1, OC]])
                                 ).then_inc(sCB.h, 16)
                    sCB.n += 16
                    A.wait_ge(sDEL.h, dtg + 1)
                    if it >= 3:
                        A.wait_ge(sSC.h, it - 2)
                    i = A.activation(dab[np_][:], delta[pp][:], AF.Exp,
                                     scale=asc[:, dt * NST + n:dt * NST + n + 1])
                    sDA.inc(i)
                    V.wait_ge(sDA.h, it + 1)
                    V.wait_ge(*dbu_rec[it])
                    if it >= 3:
                        V.wait_ge(*z_rec[it - 3])
                    i = V.tensor_tensor_scan(hb[np_][:], dab[np_][:], dbub[np_][:],
                                             state[:, dt * NST + n:dt * NST + n + 1],
                                             ALU.mult, ALU.add)
                    sSC.inc(i)
                    V.wait_ge(sSC.h, sSC.n)
                    V.tensor_copy(state[:, dt * NST + n:dt * NST + n + 1],
                                  hb[np_][:, OC - 1:OC])
                    enz = PL if pool_n(n) else V
                    sz = sZp if pool_n(n) else sZv
                    enz.wait_ge(sSC.h, it + 1)
                    enz.wait_ge(sCB.h, sCB.n)
                    if it >= 3:
                        enz.wait_ge(sY.h, it - 2)
                    i = enz.tensor_tensor(z16b[np_][:], hb[np_][:], cbc[np_][:], ALU.mult)
                    sz.inc(i)
                    z_rec[it] = (sz.h, sz.n)
                    PE.wait_ge(*z_rec[it])
                    if n == 0:
                        PE.wait_ge(sSPE.h, 4 * (dtg + 1))
                        for sub in range(4):
                            bset[sub].begin(PE)
                    for sub in range(4):
                        i = PE.matmul(bset[sub].t[:, 0:512], ident16[:],
                                      z16b[np_][:, sub * 512:(sub + 1) * 512],
                                      start=(n == 0), stop=(n == NST - 1))
                    sY.inc(i)
                    it += 1

                V.wait_ge(sY.h, it)
                V.wait_ge(sSR2.h, sSR2.n)
                if dtg >= 2:
                    V.wait_ge(sGD.h, 16 * (dtg - 1))
                for sub in range(4):
                    V.scalar_tensor_tensor(gt1[:], xi16s[pp][:, sub * 512:(sub + 1) * 512],
                                           prm[:, 16 + dt:17 + dt],
                                           bset[sub].t[:, 0:512], ALU.mult, ALU.add)
                    i = V.tensor_tensor(g16b[pp][:, sub * 512:(sub + 1) * 512], gt1[:],
                                        sr16s[pp][:, sub * 512:(sub + 1) * 512], ALU.mult)
                    sGT.inc(i)
                    bset[sub].set_free(sGT)
                A.wait_ge(sGT.h, 4 * (dtg + 1))
                i = A.dma_start(
                    out=bass.AP(cc_g_in, dt * 128 * T + oc * OC, [[T, 128], [1, OC]]),
                    in_=g16b[pp][:])
                sGD.inc(i, 16)

        PL.wait_ge(sGD.h, sGD.n)
        i = PL.collective_compute("AllGather", ALU.bypass, replica_groups=RG,
                                  ins=[cc_g_in.ap()], outs=[cc_g_out.ap()])
        sA2A.inc(i)
        ps.close()
        return dict(sGD=sGD, sGT=sGT, sY=sY, sZv=sZv, sZp=sZp, sSC=sSC)

    def emit_phase3(p2):
        for eng in (SY, A, V, PE, PL):
            eng.wait_ge(sA2A.h, 1)
            eng.wait_ge(p2["sGD"].h, p2["sGD"].n)
            eng.wait_ge(p2["sGT"].h, p2["sGT"].n)
            eng.wait_ge(p2["sY"].h, p2["sY"].n)
            eng.wait_ge(p2["sZv"].h, p2["sZv"].n)
            eng.wait_ge(p2["sZp"].h, p2["sZp"].n)

        ps = contextlib.ExitStack()
        sb = mk_sb(ps)
        wout16 = sb("wout16", [128, 16 * DM], BF16)
        wr16 = sb("wr16", [128, FT * DM], BF16)
        wk16 = sb("wk16", [128, FT * DM], BF16)
        wv16 = sb("wv16", [128, FT * DM], BF16)
        wstage = sb("wstage3", [128, DM], F32)
        sW3 = Sem("p3w"); sW3c = Sem("p3wc")
        for src, dst, nkt in ((wout_in, wout16, 16), (wr_in, wr16, FT),
                              (wk_in, wk16, FT), (wv_in, wv16, FT)):
            for kt in range(nkt):
                if sW3c.n >= 1:
                    SY.wait_ge(sW3c.h, sW3c.n)
                SY.dma_start(out=wstage[:],
                             in_=bass.AP(src, kt * 128 * DM, [[DM, 128], [1, DM]])
                             ).then_inc(sW3.h, 16)
                sW3.n += 16
                A.wait_ge(sW3.h, sW3.n)
                i = A.copy(dst[:, kt * DM:(kt + 1) * DM], wstage[:])
                sW3c.inc(i)

        g16 = sb("g16", [128, 16 * TC3], BF16)
        xtm3 = sb("xtm3", [128, 2 * 1024], F32)
        xln3 = sb("xln3", [128, 2 * 1024], F32)
        h3 = sb("h3", [128, FT * TC3], F32)
        xnew = sb("xnew", [128, FT * TC3], F32)
        xnew16 = sb("xnew16", [128, FT * TC3], BF16)
        sq16 = sb("sq16", [128, TC3], BF16)
        h2 = sb("h2", [128, FT * (TC3 + 1)], F32)
        xr16 = sb("xr16", [128, FT * TC3], BF16)
        xk16 = sb("xk16", [128, FT * TC3], BF16)
        r16 = sb("r16", [128, FT * TC3], BF16)
        kact16 = sb("kact16", [128, FT * TC3], BF16)
        fout = sb("fout", [128, FT * TC3], F32)
        outtm = sb("outtm", [128, 2 * 1024], F32)
        scr3 = sb("scr3", [128, 1024], F32)
        st3 = sb("st3", [128, 16], F32)
        ln2s = sb("ln2s", [1, 2 * TC3], F32)
        mixt = sb("mixt", [128, TC3], F32)

        sGL = Sem("p3gl"); sXL3 = Sem("p3xl"); sSQ3 = Sem("p3sq")
        sXFV3 = Sem("p3xfv"); sINV3 = Sem("p3inv")
        sXLN3 = Sem("p3xln"); sHEV3 = Sem("p3hev"); sOP = Sem("p3op")
        sAO = Sem("p3ao"); sXNW = Sem("p3xnw"); sXN16 = Sem("p3xn16")
        sSQF = Sem("p3sqf"); sMS = Sem("p3ms"); sLNI = Sem("p3lni"); sBC3 = Sem("p3bc3")
        sH2 = Sem("p3h2"); sHPC = Sem("p3hpc"); sMX = Sem("p3mx"); sRS = Sem("p3rs")
        sBCr = Sem("p3bcr")
        sKA = Sem("p3ka"); sFO = Sem("p3fo"); sOD = Sem("p3od")

        for c in range(NCH3):
            if c >= 1:
                SY.wait_ge(sOP.h, FT * c)
            SY.dma_start(out=g16[:].rearrange("p (k t) -> p k t", k=16),
                         in_=bass.AP(cc_g_out, c * TC3,
                                     [[T, 128], [EH * T, 2], [128 * T, NDT], [1, TC3]])
                         ).then_inc(sGL.h, 16)
            sGL.n += 16
            if c >= 1:
                SY.wait_ge(sXLN3.h, 2 * c)
                SY.wait_ge(sSQ3.h, 2 * c)
                SY.wait_ge(sXFV3.h, c)
                SY.wait_ge(sXNW.h, FT * c)
            SY.dma_start(out=xtm3[:].rearrange("p (s d) -> p s d", s=2),
                         in_=bass.AP(x_in, c * TC3 * DM,
                                     [[DM, 128], [128 * DM, 2], [1, DM]])
                         ).then_inc(sXL3.h, 16)
            sXL3.n += 16

            V.wait_ge(sXL3.h, sXL3.n)
            A.wait_ge(sXL3.h, sXL3.n)
            for s in range(2):
                xs = xtm3[:, s * 1024:(s + 1) * 1024]
                i = V.tensor_reduce(st3[:, s:s + 1], xs, AXX, ALU.add)
                if s == 1:
                    sXFV3.inc(i)
                i = A.activation(scr3[:], xs, AF.Square, accum_out=st3[:, 2 + s:3 + s])
                sSQ3.inc(i)
            V.wait_ge(sSQ3.h, sSQ3.n)
            if c >= 1:
                V.wait_ge(sXLN3.h, 2 * c)
            m2 = st3[:, 4:6]; v2 = st3[:, 6:8]
            i = V.tensor_scalar_mul(m2, st3[:, 0:2], 1.0 / DM)
            vlock(i)
            i = V.tensor_scalar_mul(v2, st3[:, 2:4], 1.0 / DM)
            vlock(i)
            i = V.tensor_tensor(tiny[:, 0:2], m2, m2, ALU.mult)
            vlock(i)
            i = V.tensor_tensor(v2, v2, tiny[:, 0:2], ALU.subtract)
            vlock(i)
            i = V.tensor_scalar_add(v2, v2, EPS)
            vlock(i)
            inv2 = st3[:, 0:2]
            newton_rsqrt(inv2, v2, 128, 2)
            nm2 = st3[:, 2:4]
            i = V.tensor_scalar_mul(nm2, m2, -1.0)
            vlock(i)
            i = V.tensor_tensor(nm2, nm2, inv2, ALU.mult)
            sINV3.inc(i)
            A.wait_ge(sINV3.h, sINV3.n)
            for s in range(2):
                i = A.activation(xln3[:, s * 1024:(s + 1) * 1024],
                                 xtm3[:, s * 1024:(s + 1) * 1024], AF.Identity,
                                 bias=st3[:, 2 + s:3 + s], scale=st3[:, s:s + 1])
                sXLN3.inc(i)

            PE.wait_ge(sXLN3.h, sXLN3.n)
            for j in range(FT):
                bk = banks[j % 2]
                bk.begin(PE)
                for s in range(2):
                    i = PE.transpose(bk.t[:, s * 128:(s + 1) * 128],
                                     xln3[:, s * 1024 + j * 128:s * 1024 + (j + 1) * 128],
                                     identf[:])
                bk.produced(i)
                bk.wait_data(A)
                if c >= 1:
                    A.wait_ge(sAO.h, FT * c)
                i = A.activation(h3[:, j * TC3:(j + 1) * TC3], bk.t[:, 0:TC3], AF.Identity,
                                 bias=modp[:, 0 * FT + j:0 * FT + j + 1],
                                 scale=modp[:, 1 * FT + j:1 * FT + j + 1])
                sHEV3.inc(i)
                bk.set_free(sHEV3)

                obk = banks[2 + (j % 2)]
                obk.begin(PE)
                PE.wait_ge(sGL.h, sGL.n)
                if c == 0 and j == 0:
                    PE.wait_ge(sW3c.h, 16)
                for kt in range(16):
                    i = PE.matmul(obk.t[:, 0:TC3],
                                  wout16[:, kt * DM + j * 128:kt * DM + (j + 1) * 128],
                                  g16[:, kt * TC3:(kt + 1) * TC3],
                                  start=(kt == 0), stop=(kt == 15))
                sOP.inc(i)
                obk.set_data(sOP)
                xbk = banks[4 + (j % 2)]
                xbk.begin(PE)
                for s in range(2):
                    i = PE.transpose(xbk.t[:, s * 128:(s + 1) * 128],
                                     xtm3[:, s * 1024 + j * 128:s * 1024 + (j + 1) * 128],
                                     identf[:])
                xbk.produced(i)
                obk.wait_data(V)
                xbk.wait_data(V)
                V.wait_ge(sHEV3.h, sHEV3.n)
                i = V.tensor_tensor(mixt[:, 0:TC3], obk.t[:, 0:TC3],
                                    h3[:, j * TC3:(j + 1) * TC3], ALU.add)
                sAO.inc(i)
                obk.set_free(sAO)
                i = V.scalar_tensor_tensor(xnew[:, j * TC3:(j + 1) * TC3], mixt[:, 0:TC3],
                                           modp[:, 2 * FT + j:2 * FT + j + 1],
                                           xbk.t[:, 0:TC3], ALU.mult, ALU.add)
                sXNW.inc(i)
                xbk.set_free(sXNW)
                i = V.tensor_copy(xnew16[:, j * TC3:(j + 1) * TC3],
                                  xnew[:, j * TC3:(j + 1) * TC3])
                sXN16.inc(i)

            mbk = banks[6]
            mbk.begin(PE)
            for j in range(FT):
                PE.wait_ge(sXN16.h, sXN16.n - FT + j + 1)
                i = PE.matmul(mbk.t[0:1, 0:TC3], ones16[:],
                              xnew16[:, j * TC3:(j + 1) * TC3],
                              start=(j == 0), stop=(j == FT - 1))
            mbk.produced(i)
            qbk = banks[7]
            qbk.begin(PE)
            for j in range(FT):
                A.wait_ge(sXNW.h, sXNW.n - FT + j + 1)
                if j >= 2:
                    A.wait_ge(sSQF.h, sSQF.n - 1)
                i = A.activation(sq16[:], xnew[:, j * TC3:(j + 1) * TC3], AF.Square)
                sSQF.inc(i)
                PE.wait_ge(sSQF.h, sSQF.n)
                i = PE.matmul(qbk.t[0:1, 0:TC3], ones16[:], sq16[:],
                              start=(j == 0), stop=(j == FT - 1))
            qbk.produced(i)
            mbk.wait_data(V)
            qbk.wait_data(V)
            mrow = ln2s[0:1, 0:TC3]
            vrow = ln2s[0:1, TC3:2 * TC3]
            V.tensor_scalar_mul(mrow, mbk.t[0:1, 0:TC3], 1.0 / DM)
            V.tensor_scalar_mul(vrow, qbk.t[0:1, 0:TC3], 1.0 / DM)
            V.tensor_tensor(tiny2[0:1, 0:32], mrow[0:1, 0:32], mrow[0:1, 0:32], ALU.mult) \
                if False else None
            V.tensor_tensor(nr_f[0:1, 0:TC3], mrow, mrow, ALU.mult)
            V.tensor_tensor(vrow, vrow, nr_f[0:1, 0:TC3], ALU.subtract)
            V.tensor_scalar_add(vrow, vrow, EPS)
            newton_rsqrt(vrow, vrow, 1, TC3)
            i = V.tensor_copy(tiny[0:1, 48:49], vrow[0:1, 0:1])
            sLNI.inc(i)
            mbk.set_free(sLNI)
            qbk.set_free(sLNI)
            b1 = banks[6]
            b2 = banks[7]
            b1.begin(PE)
            b2.begin(PE)
            PE.wait_ge(sLNI.h, sLNI.n)
            i = PE.matmul(b1.t[:, 0:TC3], onesf_row[:], mrow, start=True, stop=True)
            b1.produced(i)
            i = PE.matmul(b2.t[:, 0:TC3], onesf_row[:], vrow, start=True, stop=True)
            b2.produced(i)
            b1.wait_data(V)
            b2.wait_data(V)

            if c >= 1:
                A.wait_ge(sMX.h, FT * c)
                V.wait_ge(sMX.h, FT * c)
            for j in range(FT):
                if c == 0:
                    i = V.memset(h2[:, j * (TC3 + 1):j * (TC3 + 1) + 1], 0.0)
                else:
                    i = V.tensor_copy(
                        h2[:, j * (TC3 + 1):j * (TC3 + 1) + 1],
                        h2[:, j * (TC3 + 1) + TC3:j * (TC3 + 1) + TC3 + 1])
                sHPC.inc(i)
                V.tensor_tensor(mixt[:, 0:TC3], xnew[:, j * TC3:(j + 1) * TC3],
                                b1.t[:, 0:TC3], ALU.subtract)
                i = V.tensor_tensor(mixt[:, 0:TC3], mixt[:, 0:TC3], b2.t[:, 0:TC3],
                                    ALU.mult)
                if j == FT - 1:
                    sBCr.inc(i)
                    b1.set_free(sBCr)
                    b2.set_free(sBCr)
                A.wait_ge(sHPC.h, sHPC.n)
                i = A.activation(h2[:, j * (TC3 + 1) + 1:(j + 1) * (TC3 + 1)],
                                 mixt[:, 0:TC3], AF.Identity,
                                 bias=modp[:, 3 * FT + j:3 * FT + j + 1],
                                 scale=modp[:, 4 * FT + j:4 * FT + j + 1])
                sH2.inc(i)
                V.wait_ge(sH2.h, sH2.n)

            for j in range(FT):
                h2c = h2[:, j * (TC3 + 1) + 1:(j + 1) * (TC3 + 1)]
                h2p = h2[:, j * (TC3 + 1):(j + 1) * (TC3 + 1) - 1]
                V.tensor_scalar_mul(mixt[:, 0:TC3], h2c, prm[:, 32 + j:33 + j])
                V.scalar_tensor_tensor(xr16[:, j * TC3:(j + 1) * TC3], h2p,
                                       prm[:, 48 + j:49 + j], mixt[:, 0:TC3],
                                       ALU.mult, ALU.add)
                V.tensor_scalar_mul(mixt[:, 0:TC3], h2c, prm[:, 40 + j:41 + j])
                i = V.scalar_tensor_tensor(xk16[:, j * TC3:(j + 1) * TC3], h2p,
                                           prm[:, 56 + j:57 + j], mixt[:, 0:TC3],
                                           ALU.mult, ALU.add)
                sMX.inc(i)

            for j in range(FT):
                rbk = banks[2 + (j % 2)]
                rbk.begin(PE)
                PE.wait_ge(sMX.h, sMX.n)
                for kt in range(FT):
                    i = PE.matmul(rbk.t[:, 0:TC3],
                                  wr16[:, kt * DM + j * 128:kt * DM + (j + 1) * 128],
                                  xr16[:, kt * TC3:(kt + 1) * TC3],
                                  start=(kt == 0), stop=(kt == FT - 1))
                rbk.produced(i)
                rbk.wait_data(A)
                i = A.activation(r16[:, j * TC3:(j + 1) * TC3], rbk.t[:, 0:TC3],
                                 AF.Sigmoid)
                sRS.inc(i)
                rbk.set_free(sRS)
            for j in range(FT):
                kbk = banks[4 + (j % 2)]
                kbk.begin(PE)
                for kt in range(FT):
                    i = PE.matmul(kbk.t[:, 0:TC3],
                                  wk16[:, kt * DM + j * 128:kt * DM + (j + 1) * 128],
                                  xk16[:, kt * TC3:(kt + 1) * TC3],
                                  start=(kt == 0), stop=(kt == FT - 1))
                kbk.produced(i)
                kbk.wait_data(A)
                i = A.activation(scr3[:, 0:TC3], kbk.t[:, 0:TC3], AF.Relu)
                i = A.activation(kact16[:, j * TC3:(j + 1) * TC3], scr3[:, 0:TC3],
                                 AF.Square)
                sKA.inc(i)
                kbk.set_free(sKA)
            for j in range(FT):
                vbk = banks[2 + (j % 2)]
                vbk.begin(PE)
                PE.wait_ge(sKA.h, sKA.n)
                for kt in range(FT):
                    i = PE.matmul(vbk.t[:, 0:TC3],
                                  wv16[:, kt * DM + j * 128:kt * DM + (j + 1) * 128],
                                  kact16[:, kt * TC3:(kt + 1) * TC3],
                                  start=(kt == 0), stop=(kt == FT - 1))
                vbk.produced(i)
                vbk.wait_data(V)
                V.wait_ge(sRS.h, sRS.n - FT + j + 1)
                i = V.tensor_tensor(mixt[:, 0:TC3], r16[:, j * TC3:(j + 1) * TC3],
                                    vbk.t[:, 0:TC3], ALU.mult)
                vbk.done(i)
                i = V.scalar_tensor_tensor(fout[:, j * TC3:(j + 1) * TC3], mixt[:, 0:TC3],
                                           modp[:, 5 * FT + j:5 * FT + j + 1],
                                           xnew[:, j * TC3:(j + 1) * TC3],
                                           ALU.mult, ALU.add)
                sFO.inc(i)

            for s in range(2):
                for hhalf in range(2):
                    tbk = banks[6 + hhalf]
                    tbk.begin(PE)
                    PE.wait_ge(sFO.h, sFO.n)
                    for jj in range(4):
                        j = hhalf * 4 + jj
                        i = PE.transpose(tbk.t[:, jj * 128:(jj + 1) * 128],
                                         h3[:, j * TC3 + s * 128:j * TC3 + s * 128 + 128]
                                         if False else
                                         fout[:, j * TC3 + s * 128:j * TC3 + s * 128 + 128],
                                         identf[:])
                    tbk.produced(i)
                    tbk.wait_data(A)
                    if c >= 1:
                        A.wait_ge(sOD.h, 16 * c)
                    i = A.activation(outtm[:, s * 1024 + hhalf * 512:
                                           s * 1024 + (hhalf + 1) * 512],
                                     tbk.t[:, 0:512], AF.Copy)
                    tbk.done(i)
            i = A.dma_start(out=bass.AP(out_t, c * TC3 * DM,
                                        [[DM, 128], [128 * DM, 2], [1, DM]]),
                            in_=outtm[:].rearrange("p (s d) -> p s d", s=2))
            sOD.inc(i, 16)

        if DEBUG_TAPS:
            sDBG = sDBGG
            for nm, srct in (("d_mod", modrow_d), ("d_xdbl_in", cc_xdbl_in),
                             ("d_xdbl_out", cc_xdbl_out), ("d_xi", spill_xi),
                             ("d_sr", spill_sr), ("d_gin", cc_g_in),
                             ("d_gout", cc_g_out)):
                i = SY.dma_start(out=dbg[nm].ap(), in_=srct.ap())
                sDBG.inc(i, 16)
            SY.wait_ge(sDBG.h, sDBG.n)
        SY.wait_ge(sOD.h, 16 * NCH3)
        nc.all_engine_barrier()
        SY.sem_clear(range(155, 256))
        nc.all_engine_barrier()
        ps.close()

    SY.sem_clear(range(155, 256))
    nc.all_engine_barrier()
    emit_prologue()
    p1 = emit_phase1()
    p2 = emit_phase2(p1)
    emit_phase3(p2)

    ges.close()
    return nc


def _percore_inputs(inputs, c):
    b, h = c // 2, c % 2
    f = np.float32
    x = np.ascontiguousarray(inputs["L0_attn_x"][b]).astype(f, copy=False)
    in_proj = inputs["L0_attn_in_proj_w"]
    w_in = np.ascontiguousarray(
        np.concatenate([in_proj[:, h * EH:(h + 1) * EH],
                        in_proj[:, 2048 + h * EH:2048 + (h + 1) * EH]], axis=1))
    conv_w = inputs["L0_attn_conv_w"][h * EH:(h + 1) * EH, 0, :]
    conv_diag = np.zeros((NDT * 4, 128, 128), f)
    for dt in range(NDT):
        for tap in range(4):
            np.fill_diagonal(conv_diag[dt * 4 + tap], conv_w[dt * 128:(dt + 1) * 128, tap])
    prm = np.zeros((128, 64), f)

    def col8(vec, base):
        prm[:, base:base + 8] = vec.reshape(8, 128).T

    col8(inputs["L0_attn_conv_b"][h * EH:(h + 1) * EH], 0)
    col8(inputs["L0_attn_dt_proj_b"][h * EH:(h + 1) * EH], 8)
    col8(inputs["L0_attn_D"][h * EH:(h + 1) * EH], 16)
    col8(inputs["L0_attn_norm_w"], 24)
    col8(inputs["L0_ffn_mu_r"], 32)
    col8(inputs["L0_ffn_mu_k"], 40)
    col8(1.0 - inputs["L0_ffn_mu_r"], 48)
    col8(1.0 - inputs["L0_ffn_mu_k"], 56)
    alog = inputs["L0_attn_A_log"][h * EH:(h + 1) * EH]
    a_log = np.zeros((128, 128), f)
    for dt in range(NDT):
        a_log[:, dt * NST:(dt + 1) * NST] = alog[dt * 128:(dt + 1) * 128, :]
    xpw = np.ascontiguousarray(inputs["L0_attn_x_proj_w"][h * EH:(h + 1) * EH, :])
    dt_w = np.ascontiguousarray(inputs["L0_attn_dt_proj_w"][:, h * EH:(h + 1) * EH])
    cvec = np.ascontiguousarray(inputs["L0_c"][b].reshape(8, 128).T)
    return {
        "x": x,
        "w_in": w_in.astype(f, copy=False),
        "conv_diag": conv_diag,
        "prm": prm,
        "a_log": a_log.astype(f, copy=False),
        "x_proj_w": xpw.astype(f, copy=False),
        "dt_w": dt_w.astype(f, copy=False),
        "w_out": np.ascontiguousarray(inputs["L0_attn_out_proj_w"]).astype(f, copy=False),
        "w_r": np.ascontiguousarray(inputs["L0_ffn_W_r"].T).astype(f, copy=False),
        "w_k": np.ascontiguousarray(inputs["L0_ffn_W_k"].T).astype(f, copy=False),
        "w_v": np.ascontiguousarray(inputs["L0_ffn_W_v"].T).astype(f, copy=False),
        "cvec": cvec,
        "wmod": np.ascontiguousarray(inputs["L0_wmod"]).astype(f, copy=False),
        "ident": np.eye(128, dtype=f),
        "magic": np.full((128, 512), MAGIC, dtype=np.int32),
    }


def kernel(**inputs):
    nc = build_program()
    in_maps = [_percore_inputs(inputs, c) for c in range(8)]
    res = run_bass_kernel_spmd(nc, in_maps, core_ids=list(range(8)))
    out = np.empty((B, T, DM), np.float32)
    for c in range(8):
        b, h = c // 2, c % 2
        out[b, h * TH:(h + 1) * TH, :] = res.results[c]["out"][h * TH:(h + 1) * TH]
    return out
